# version 23
# baseline (speedup 1.0000x reference)
import contextlib
import numpy as np
import ml_dtypes
import concourse.bass as bass
import concourse.mybir as mybir
from concourse.bass_utils import run_bass_kernel_spmd

F32 = mybir.dt.float32
BF16 = mybir.dt.bfloat16
ALU = mybir.AluOpType
ACT = mybir.ActivationFunctionType
AX = mybir.AxisListType
bf16 = ml_dtypes.bfloat16

T = 2048
M = 2048
FF = 8192
H = 16
D = 128
NT = 16
NQ = 4
QT = 4
NFCC = 16
NFC = 64
NG = 4
GH = 4

MAGIC_K = 1.5 * 2.0 ** 82
MAGIC_C0 = 1.5 * 2.0 ** 23


class Sched:

    def __init__(self):
        self.ops = []
        self.counts = {}
        self.var = {}

    def n(self, sem):
        return self.counts.get(id(sem), 0)

    def op(self, eng, fn, waits=(), inc=None, amt=1, drain=False):
        incs = []
        if inc is not None:
            self.counts[id(inc)] = self.n(inc) + amt
            incs = [(inc, amt)]
        ws = tuple((s, v) for (s, v) in waits if v > 0)
        self.ops.append((eng, fn, ws, incs, drain))

    def emit(self, name, eng):
        seen = {}
        for (en, fn, waits, incs, drain) in self.ops:
            if en != name:
                continue
            for (sem, v) in waits:
                if seen.get(id(sem), 0) < v:
                    eng.wait_ge(sem, v)
                    seen[id(sem)] = v
            if drain:
                eng.drain()
            instr = fn(eng)
            for (sem, k) in incs:
                instr.then_inc(sem, k)


def build_nc():
    nc = bass.Bass("TRN2", target_bir_lowering=False, dynamic_dma_scratch_size=16384)

    def din(name, shape, dt):
        return nc.dram_tensor(name, shape, dt, kind="ExternalInput").ap()

    def dint(name, shape, dt):
        return nc.dram_tensor(name, shape, dt, kind="Internal").ap()

    x_in = din("x", [T, M], F32)
    wq_in = din("wq", [M, H, D], F32)
    wk_in = din("wk", [M, H, D], F32)
    wv_in = din("wv", [M, H, D], F32)
    wo_in = din("wo", [H, D, M], F32)
    w1t_in = din("w1t", [M, FF], F32)
    w2t_in = din("w2t", [M, FF], F32)
    w3t_in = din("w3t", [FF, M], F32)
    ident_in = din("ident", [128, 128], BF16)
    perm_in = din("perm", [128, 128], F32)
    mask_in = din("mask", [128, 128], F32)
    cch_in = din("cch", [128, T], F32)
    ssh_in = din("ssh", [128, T], F32)

    t1s = dint("t1s", [M, FF], BF16)
    t2s = dint("t2s", [M, FF], BF16)
    t3s = dint("t3s", [FF, M], BF16)
    x1s = dint("x1s", [T, M], F32)
    x1s2 = dint("x1s2", [T, M], F32)
    ln2s = dint("ln2s", [M, T], BF16)

    out_d = nc.dram_tensor("out", [T, M], F32, kind="ExternalOutput").ap()

    t1v = t1s.rearrange("(a p) f -> p a f", p=128)
    t2v = t2s.rearrange("(a p) f -> p a f", p=128)
    ln2v = ln2s.rearrange("(a p) t -> p a t", p=128)

    es = contextlib.ExitStack()
    with es:
        def sbt(name, cols, dt):
            return es.enter_context(nc.sbuf_tensor(name, [128, cols], dt))

        A16 = 65536
        A32 = 10240
        a16 = sbt("a16", A16, BF16)
        a32 = sbt("a32", A32, F32)
        cc_t = sbt("cc_t", T, F32)
        ss_t = sbt("ss_t", T, F32)
        ident = sbt("identsb", 128, BF16)
        perm = sbt("permsb", 128, F32)
        mask = sbt("masksb", 128, F32)
        stat = sbt("stat", 128, F32)
        part = sbt("part", 256, F32)
        ones = sbt("ones", 128, F32)

        ps1 = es.enter_context(nc.psum_tensor("ps1", [128, 2048], F32))
        ps2 = es.enter_context(nc.psum_tensor("ps2", [128, 1024], F32))
        psT = es.enter_context(nc.psum_tensor("psT", [128, 2048], BF16))

        dW = es.enter_context(nc.semaphore("dW"))
        dX = es.enter_context(nc.semaphore("dX"))
        dG = es.enter_context(nc.semaphore("dG"))
        dO = es.enter_context(nc.semaphore("dO"))
        smm = es.enter_context(nc.semaphore("smm"))
        svv = es.enter_context(nc.semaphore("svv"))
        sss = es.enter_context(nc.semaphore("sss"))
        sgg = es.enter_context(nc.semaphore("sgg"))

        block = es.enter_context(nc.Block())
        S = Sched()

        _si = [0]

        def stcol():
            i = _si[0]
            _si[0] += 1
            assert _si[0] <= 128
            return stat[:, i:i + 1]

        C_ap = stcol()
        flag = stcol()
        c127 = stcol()
        eps5 = stcol()
        eps6 = stcol()
        one_c = stcol()
        rs_w = [stcol() for _ in range(3)]
        sw_w = [stcol() for _ in range(3)]
        g1c = [stcol() for _ in range(QT)]
        g2c = [stcol() for _ in range(QT)]
        c1c = [stcol() for _ in range(QT)]
        c2c = [stcol() for _ in range(QT)]
        c1h = [stcol() for _ in range(QT)]
        coef3 = [stcol() for _ in range(QT)]
        invc = stcol()
        c2a = stcol()
        rden = stcol()
        pidx = stcol()
        jv = stcol()
        sgn_c = stcol()
        fr_c = stcol()
        tA = [stcol() for _ in range(8)]

        ssqp = [part[:, i * NFCC:(i + 1) * NFCC] for i in range(QT)]
        hmxp = [part[:, 64 + i * NFCC:64 + (i + 1) * NFCC] for i in range(QT)]
        wpart = part[:, 128:192]
        den_c = part[:, 192:193]

        A_h = 0
        A_nT = 32768
        A_w1 = 40960
        A_w2 = 57344
        A_nscr = 57344
        A_tern = 61440
        P_K = 0
        P_V = 8192
        P_Q = 16384
        P_wo = 24576
        P_wqkv = 32768
        P_ln2c = 45056
        P_Pa = 32768
        P_Pb = 34816
        P_PT = 36864
        P_O = 38912
        P_vt = 61440
        P_lt = 61440

        B_x = 0
        B_vq = 2048
        B_dump = 4096
        B_sil = 5120
        B_rx = 6144
        B_kq = 2048
        B_x1 = 0

        def a16s(off, n):
            return a16[:, off:off + n]

        def a32s(off, n):
            return a32[:, off:off + n]

        S.op("sync", lambda e: e.dma_start(out=ident[:], in_=ident_in[:]), inc=dW, amt=16)
        S.op("sync", lambda e: e.dma_start(out=perm[:], in_=perm_in[:]), inc=dW, amt=16)
        S.op("sync", lambda e: e.dma_start(out=mask[:], in_=mask_in[:]), inc=dW, amt=16)
        S.op("sync", lambda e: e.dma_start(out=cc_t[:], in_=cch_in[:]), inc=dW, amt=16)
        S.op("sync", lambda e: e.dma_start(out=ss_t[:], in_=ssh_in[:]), inc=dW, amt=16)
        w0 = S.n(dW)

        S.op("vector", lambda e: e.memset(tA[0], 2.0 ** -30))
        S.op("vector", lambda e: e.tensor_scalar(tA[1], tA[0], 1.0, None, ALU.add), drain=True)
        S.op("vector", lambda e: e.tensor_scalar(tA[2], tA[1], -1.0, None, ALU.add), drain=True)
        S.op("vector", lambda e: e.tensor_scalar(C_ap, tA[2], MAGIC_K, MAGIC_C0, ALU.mult, ALU.add), drain=True)
        S.op("vector", lambda e: e.tensor_scalar(flag, tA[2], 2.0 ** 30, None, ALU.mult), drain=True)
        S.op("vector", lambda e: e.memset(c127, 127.0))
        S.op("vector", lambda e: e.memset(eps5, 1e-5))
        S.op("vector", lambda e: e.memset(eps6, 1e-6))
        S.op("vector", lambda e: e.memset(one_c, 1.0))
        S.op("vector", lambda e: e.memset(ones[:], 1.0), inc=svv)

        rope_rad = a32s(0, 2048)
        rope_t1 = a32s(2048, 2048)
        rope_t2 = a32s(4096, 2048)

        S.op("gpsimd", lambda e: e.iota(pidx, pattern=[[0, 1]], base=0, channel_multiplier=1,
                                        allow_small_or_imprecise_dtypes=True), inc=sgg)
        S.op("gpsimd", lambda e: e.iota(rope_rad[:], pattern=[[1, 2048]], base=0, channel_multiplier=0,
                                        allow_small_or_imprecise_dtypes=True), inc=sgg)
        g0 = S.n(sgg)
        S.op("vector", lambda e: e.tensor_scalar(jv, pidx, 64.0, None, ALU.is_ge),
             waits=[(sgg, g0)], drain=True)
        S.op("vector", lambda e: e.tensor_scalar(sgn_c, jv, 2.0, -1.0, ALU.mult, ALU.add), drain=True)
        S.op("vector", lambda e: e.scalar_tensor_tensor(jv, jv, -64.0, pidx, ALU.mult, ALU.add),
             inc=svv, drain=True)
        v_j = S.n(svv)
        S.op("scalar", lambda e: e.activation(tA[0], one_c, ACT.Ln, scale=10000.0), waits=[(svv, v_j)], inc=sss)
        s_ln = S.n(sss)
        S.op("vector", lambda e: e.tensor_scalar(tA[0], tA[0], -1.0 / 64.0, None, ALU.mult),
             waits=[(sss, s_ln)], inc=svv, drain=True)
        S.op("scalar", lambda e: e.activation(fr_c, jv, ACT.Exp, scale=tA[0]),
             waits=[(svv, S.n(svv))], inc=sss)
        s_fr = S.n(sss)
        S.op("scalar", lambda e: e.activation(tA[4], one_c, ACT.Arctan), inc=sss)
        s_at = S.n(sss)
        S.op("vector", lambda e: e.tensor_scalar(tA[4], tA[4], 4.0, None, ALU.mult), waits=[(sss, s_at)], drain=True)
        S.op("vector", lambda e: e.tensor_scalar(tA[3], tA[4], -1.0, None, ALU.mult), drain=True)
        S.op("vector", lambda e: e.tensor_scalar(tA[6], tA[4], 0.5, None, ALU.mult), drain=True)
        S.op("vector", lambda e: e.tensor_scalar(tA[7], tA[4], -2.0, None, ALU.mult), drain=True)
        S.op("vector", lambda e: e.tensor_scalar(tA[5], tA[4], 2.0, None, ALU.mult), drain=True)
        S.op("vector", lambda e: e.reciprocal(tA[5], tA[5]), drain=True)
        S.op("vector", lambda e: e.tensor_scalar(rope_rad[:], rope_rad[:], fr_c, None, ALU.mult),
             waits=[(sss, s_fr)], inc=svv, drain=True)

        def rope_dev(tbl, phase_half_pi):
            if phase_half_pi:
                S.op("vector", lambda e: e.tensor_scalar(rope_t1[:], rope_rad[:], tA[6], None, ALU.add), drain=True)
            else:
                S.op("vector", lambda e: e.tensor_copy(rope_t1[:], rope_rad[:]), drain=True)
            S.op("vector", lambda e: e.tensor_scalar(rope_t2[:], rope_t1[:], tA[5], None, ALU.mult), drain=True)
            S.op("vector", lambda e: e.tensor_scalar(rope_t2[:], rope_t2[:], C_ap, C_ap, ALU.add, ALU.subtract), drain=True)
            S.op("vector", lambda e: e.tensor_scalar(rope_t2[:], rope_t2[:], tA[7], None, ALU.mult), drain=True)
            S.op("vector", lambda e: e.tensor_tensor(rope_t2[:], rope_t2[:], rope_t1[:], ALU.add), drain=True)
            S.op("vector", lambda e: e.tensor_scalar(rope_t2[:], rope_t2[:], tA[4], None, ALU.min), drain=True)
            S.op("vector", lambda e: e.tensor_scalar(rope_t2[:], rope_t2[:], tA[3], None, ALU.max),
                 inc=svv, drain=True)
            vv0 = S.n(svv)
            S.op("scalar", lambda e: e.activation(rope_t1[:], rope_t2[:], ACT.Sin), waits=[(svv, vv0)], inc=sss)
            ss0 = S.n(sss)
            S.op("vector", lambda e, t=tbl: e.tensor_tensor(rope_t2[:], rope_t1[:], t, ALU.subtract),
                 waits=[(sss, ss0), (dW, w0)], drain=True)
            S.op("vector", lambda e, t=tbl: e.scalar_tensor_tensor(rope_t1[:], rope_t2[:], flag, t, ALU.mult, ALU.add),
                 drain=True)
            S.op("vector", lambda e, t=tbl: e.tensor_copy(t, rope_t1[:]), inc=svv, drain=True)

        rope_dev(ss_t[:], False)
        rope_dev(cc_t[:], True)
        S.op("vector", lambda e: e.tensor_scalar(ss_t[:], ss_t[:], sgn_c, None, ALU.mult), inc=svv, drain=True)
        v_rope = S.n(svv)

        wsrc = [w1t_in, w2t_in, w3t_in]
        tdst = [t1s, t2s, t3s]
        NCH = 32

        def chunk_ap(t, idx, wi):
            cols = FF if wi < 2 else M
            ccw = min(4096, cols)
            per = cols // ccw
            r, c = idx // per, idx % per
            return t[r * 128:(r + 1) * 128, c * ccw:(c + 1) * ccw]

        sc_state = {}

        def plan_scale_chunk(wi, idx, lastuse, loads):
            par = idx % 2
            buf = a32s(par * 4096, 4096)
            ccw = 4096 if wi < 2 else 2048
            buf = a32s(par * 4096, ccw)
            S.op("sync", lambda e, wi=wi, idx=idx, buf=buf:
                 e.dma_start(out=buf, in_=chunk_ap(wsrc[wi], idx, wi)),
                 waits=[(svv, lastuse[par])], inc=dW, amt=16)
            wd = S.n(dW)
            col = idx % 64
            S.op("vector", lambda e, buf=buf, col=col: e.reduce_sum(
                wpart[:, col:col + 1], buf, AX.X, apply_absolute_value=True),
                waits=[(dW, wd)], inc=svv)
            lastuse[par] = S.n(svv)

        def plan_scale_final(wi):
            n = 32 if wi < 2 else 64
            S.op("vector", lambda e, n=n: e.reduce_sum(tA[1], wpart[:, 0:n], AX.X), inc=svv, drain=True)
            vsum = S.n(svv)
            S.op("tensor", lambda e: e.matmul(ps2[:, 0:1], ones[:], tA[1], start=True, stop=True),
                 waits=[(svv, vsum)], inc=smm)
            mm0 = S.n(smm)
            S.op("vector", lambda e, wi=wi: e.tensor_scalar(rs_w[wi], ps2[:, 0:1], 1.0 / 16777216.0, None, ALU.mult),
                 waits=[(smm, mm0)], drain=True)
            S.op("vector", lambda e, wi=wi: e.tensor_scalar(rs_w[wi], rs_w[wi], 1e-5, None, ALU.max), drain=True)
            S.op("vector", lambda e, wi=wi: e.reciprocal(sw_w[wi], rs_w[wi]), inc=svv, drain=True)

        def plan_tern_chunk(wi, idx, lastuse, laststore):
            par = idx % 2
            fbuf = a32s(8192 + par * 1024, 1024)
            tbuf = a16s(A_tern + par * 1024, 1024)
            rows, cols = (M, FF) if wi < 2 else (FF, M)
            per = cols // 1024
            r, c = idx // per, idx % per
            S.op("sync", lambda e, wi=wi, r=r, c=c, fbuf=fbuf:
                 e.dma_start(out=fbuf, in_=wsrc[wi][r * 128:(r + 1) * 128, c * 1024:(c + 1) * 1024]),
                 waits=[(svv, lastuse[par]), (dG, laststore[par])], inc=dW, amt=16)
            wd = S.n(dW)
            S.op("vector", lambda e, fbuf=fbuf, wi=wi:
                 e.tensor_scalar(fbuf, fbuf, sw_w[wi], C_ap, ALU.mult, ALU.add),
                 waits=[(dW, wd)], drain=True)
            S.op("vector", lambda e, fbuf=fbuf: e.tensor_scalar(fbuf, fbuf, C_ap, None, ALU.subtract),
                 drain=True)
            S.op("vector", lambda e, fbuf=fbuf, tbuf=tbuf:
                 e.tensor_scalar(tbuf, fbuf, 1.0, -1.0, ALU.min, ALU.max), inc=svv, drain=True)
            vv = S.n(svv)
            lastuse[par] = vv
            S.op("gpsimd", lambda e, wi=wi, r=r, c=c, tbuf=tbuf:
                 e.dma_start(out=tdst[wi][r * 128:(r + 1) * 128, c * 1024:(c + 1) * 1024], in_=tbuf),
                 waits=[(svv, vv)], inc=dG, amt=16)
            laststore[par] = S.n(dG)

        sc_use = [max(S.n(svv), 1), max(S.n(svv), 1)]
        for idx in range(32):
            plan_scale_chunk(0, idx, sc_use, None)
        plan_scale_final(0)
        tn_use = [S.n(svv), S.n(svv)]
        tn_st = [0, 0]
        for wi in range(3):
            nsc = 32 if wi == 0 else 64
            step = 128 // nsc
            for idx in range(128):
                plan_tern_chunk(wi, idx, tn_use, tn_st)
                if wi < 2 and idx % step == 0 and idx // step < nsc:
                    plan_scale_chunk(wi + 1, idx // step, sc_use, None)
            if wi < 2:
                plan_scale_final(wi + 1)
            if wi == 0:
                g_t1 = S.n(dG)
            elif wi == 1:
                g_t2 = S.n(dG)
        g_tern = S.n(dG)
        g_t3 = g_tern

        def ffn_pass(xsrc, ydst, tag):
            V = {"xfree": S.n(svv),
                 "rx0": S.n(svv), "rx1": S.n(svv), "b1ev": S.n(svv),
                 "rxst0": S.n(dO), "rxst1": S.n(dO), "ring": 0,
                 "t1free": S.n(smm), "t2free": S.n(smm),
                 "bfree_v": [S.n(svv)] * 6, "bfree_s": [S.n(sss)] * 6,
                 "silfree": [S.n(svv)] * 4}
            t3last = [S.n(smm)] * 4
            for q in range(NQ):
                aprep_v = 0
                aprep_s = 0
                for t in range(QT):
                    tti = q * QT + t
                    xbuf = a32s(B_x, 2048)
                    S.op("scalar", lambda e, xsrc=xsrc, tti=tti, xbuf=xbuf:
                         e.dma_start(out=xbuf, in_=xsrc[tti * 128:(tti + 1) * 128, :]),
                         waits=[(svv, V["xfree"])], inc=dX, amt=16)
                    xd = S.n(dX)
                    S.op("vector", lambda e, xbuf=xbuf: e.reduce_sum(tA[0], xbuf, AX.X),
                         waits=[(dX, xd)], drain=True)
                    S.op("scalar", lambda e, xbuf=xbuf: e.activation(a32s(B_dump, 1024), xbuf[:, 0:1024],
                         ACT.Square, accum_out=tA[5]), waits=[(dX, xd)])
                    S.op("scalar", lambda e, xbuf=xbuf: e.activation(a32s(B_dump, 1024), xbuf[:, 1024:2048],
                         ACT.Square, accum_out=tA[6]), inc=sss)
                    sq0 = S.n(sss)
                    S.op("vector", lambda e, xbuf=xbuf: e.reduce_max(tA[2], xbuf, AX.X), drain=True)
                    S.op("vector", lambda e, xbuf=xbuf: e.tensor_reduce(tA[3], xbuf, AX.X, ALU.min), drain=True)
                    S.op("vector", lambda e: e.tensor_scalar(tA[0], tA[0], 1.0 / M, None, ALU.mult),
                         waits=[(sss, sq0)], drain=True)
                    S.op("vector", lambda e: e.tensor_tensor(tA[5], tA[5], tA[6], ALU.add), drain=True)
                    S.op("vector", lambda e: e.tensor_scalar(tA[5], tA[5], 1.0 / M, None, ALU.mult), drain=True)
                    S.op("vector", lambda e: e.tensor_scalar(tA[1], tA[0], tA[0], None, ALU.mult), drain=True)
                    S.op("vector", lambda e: e.scalar_tensor_tensor(tA[1], tA[1], -1.0, tA[5], ALU.mult, ALU.add), drain=True)
                    S.op("vector", lambda e: e.tensor_tensor(tA[2], tA[2], tA[0], ALU.subtract), drain=True)
                    S.op("vector", lambda e: e.tensor_tensor(tA[3], tA[0], tA[3], ALU.subtract), drain=True)
                    S.op("vector", lambda e: e.tensor_tensor(tA[2], tA[2], tA[3], ALU.max), drain=True)
                    S.op("vector", lambda e, t=t: e.reciprocal(c1c[t], tA[2]), drain=True)
                    S.op("vector", lambda e, t=t: e.tensor_scalar(c1c[t], c1c[t], 127.0, None, ALU.mult), drain=True)
                    S.op("vector", lambda e, t=t: e.tensor_scalar(c2c[t], tA[0], c1c[t], -1.0, ALU.mult, ALU.mult), drain=True)
                    S.op("vector", lambda e: e.tensor_tensor(tA[4], tA[1], eps5, ALU.add), inc=svv, drain=True)
                    vst = S.n(svv)
                    S.op("scalar", lambda e: e.activation(tA[4], tA[4], ACT.Ln), waits=[(svv, vst)])
                    S.op("scalar", lambda e: e.activation(tA[4], tA[4], ACT.Exp, scale=-0.5), inc=sss)
                    sst = S.n(sss)
                    S.op("vector", lambda e: e.tensor_scalar(tA[6], tA[4], tA[4], None, ALU.mult),
                         waits=[(sss, sst)], drain=True)
                    S.op("vector", lambda e: e.tensor_tensor(tA[6], tA[1], tA[6], ALU.mult), drain=True)
                    S.op("vector", lambda e: e.tensor_tensor(tA[6], tA[6], eps6, ALU.add), inc=svv, drain=True)
                    vst = S.n(svv)
                    S.op("scalar", lambda e: e.activation(tA[6], tA[6], ACT.Ln), waits=[(svv, vst)])
                    S.op("scalar", lambda e: e.activation(tA[6], tA[6], ACT.Exp, scale=-0.5), inc=sss)
                    sst = S.n(sss)
                    S.op("vector", lambda e: e.tensor_scalar(tA[7], tA[2], tA[4], None, ALU.mult),
                         waits=[(sss, sst)], drain=True)
                    S.op("vector", lambda e: e.tensor_scalar(tA[7], tA[7], tA[6], None, ALU.mult), drain=True)
                    S.op("vector", lambda e: e.tensor_scalar(tA[7], tA[7], 1e-5, 1.0 / 127.0, ALU.max, ALU.mult), drain=True)
                    S.op("vector", lambda e, t=t: e.tensor_scalar(g1c[t], tA[7], rs_w[0], None, ALU.mult), drain=True)
                    S.op("vector", lambda e, t=t: e.tensor_scalar(g2c[t], tA[7], rs_w[1], None, ALU.mult), inc=svv, drain=True)
                    for hh in range(2):
                        vq = a32s(B_vq + hh * 1024, 1024)
                        nq = a16s(A_nscr + (t % 2) * 2048 + hh * 1024, 1024)
                        S.op("vector", lambda e, xbuf=xbuf, vq=vq, t=t, hh=hh:
                             e.tensor_scalar(vq, xbuf[:, hh * 1024:(hh + 1) * 1024], c1c[t], c2c[t], ALU.mult, ALU.add),
                             drain=True)
                        S.op("vector", lambda e, vq=vq, nq=nq:
                             e.tensor_scalar(nq, vq, C_ap, C_ap, ALU.add, ALU.subtract), inc=svv, drain=True)
                    V["xfree"] = S.n(svv)
                    vq_done = S.n(svv)
                    nq_full = a16s(A_nscr + (t % 2) * 2048, 2048)
                    for mc in range(16):
                        S.op("tensor", lambda e, nq=nq_full, mc=mc:
                             e.transpose(psT[:, mc * 128:(mc + 1) * 128], nq[:, mc * 128:(mc + 1) * 128], ident[:]),
                             waits=[(svv, vq_done)] if mc == 0 else [],
                             inc=smm if mc == 15 else None)
                    tmm = S.n(smm)
                    nT_view = a16s(A_nT, 8192).rearrange("p (a t) -> p a t", t=512)
                    S.op("vector", lambda e, t=t, nv=nT_view: e.tensor_copy(
                        nv[:, :, t * 128:(t + 1) * 128], psT[:, 0:2048]),
                        waits=[(smm, tmm)], inc=svv)
                aprep_v = S.n(svv)
                aprep_s = S.n(sss)

                for fcc in range(NFCC):
                    t1b = a16s(A_w1, 8192)
                    t2b = a16s(A_w1 + 8192, 8192)
                    S.op("sync", lambda e, fcc=fcc:
                         e.dma_start(out=a16s(A_w1, 8192).rearrange("p (a f) -> p a f", f=512),
                                     in_=t1v[:, :, fcc * 512:(fcc + 1) * 512]),
                         waits=[(smm, V["t1free"]), (dG, g_t1)], inc=dW, amt=16)
                    w1d = S.n(dW)
                    S.op("sync", lambda e, fcc=fcc:
                         e.dma_start(out=a16s(A_w1 + 8192, 8192).rearrange("p (a f) -> p a f", f=512),
                                     in_=t2v[:, :, fcc * 512:(fcc + 1) * 512]),
                         waits=[(smm, V["t2free"]), (dG, g_t2)], inc=dW, amt=16)
                    w2d = S.n(dW)
                    def bank_ap(b):
                        return ps1[:, b * 512:(b + 1) * 512] if b < 4 else ps2[:, (b - 4) * 512:(b - 3) * 512]
                    u1bank = [(V["ring"] + i) % 6 for i in range(4)]
                    u2bank = [(V["ring"] + 4 + i) % 6 for i in range(4)]
                    V["ring"] = (V["ring"] + 8) % 6
                    mmu1 = [0] * 4
                    for t in range(QT):
                        b = u1bank[t]
                        pu1 = bank_ap(b)
                        for mc in range(16):
                            lhs = a16s(A_nT + mc * 512 + t * 128, 128)
                            S.op("tensor", lambda e, lhs=lhs, t1b=t1b, mc=mc, pu1=pu1:
                                 e.matmul(pu1, lhs, t1b[:, mc * 512:(mc + 1) * 512],
                                          start=(mc == 0), stop=(mc == 15), skip_group_check=True),
                                 waits=([(dW, w1d), (svv, max(aprep_v, V["bfree_v"][b])),
                                         (sss, V["bfree_s"][b])] if mc == 0 else []),
                                 inc=smm if mc == 15 else None)
                        mmu1[t] = S.n(smm)
                    V["t1free"] = S.n(smm)
                    for t in range(QT):
                        b = u2bank[t]
                        pu2 = bank_ap(b)
                        for mc in range(16):
                            lhs = a16s(A_nT + mc * 512 + t * 128, 128)
                            S.op("tensor", lambda e, lhs=lhs, t2b=t2b, mc=mc, pu2=pu2:
                                 e.matmul(pu2, lhs, t2b[:, mc * 512:(mc + 1) * 512],
                                          start=(mc == 0), stop=(mc == 15), skip_group_check=True),
                                 waits=([(dW, w2d), (svv, V["bfree_v"][b]), (sss, V["bfree_s"][b])]
                                        if mc == 0 else []),
                                 inc=smm if mc == 15 else None)
                        mm_u2 = S.n(smm)
                        silb = a32s([B_sil, B_sil + 512, 7168, 7680][t], 512)
                        pu1 = bank_ap(u1bank[t])
                        S.op("scalar", lambda e, pu1=pu1, silb=silb, t=t:
                             e.activation(silb, pu1, ACT.Silu, scale=g1c[t]),
                             waits=[(smm, mmu1[t]), (sss, aprep_s), (svv, V["silfree"][t])], inc=sss)
                        V["bfree_s"][u1bank[t]] = S.n(sss)
                        s_sil = S.n(sss)
                        hslice = a16s(A_h + t * 8192 + fcc * 512, 512)
                        S.op("vector", lambda e, silb=silb, pu2=pu2, hs=hslice:
                             e.tensor_tensor(hs, silb, pu2, ALU.mult),
                             waits=[(sss, s_sil), (smm, mm_u2)], inc=svv)
                        V["bfree_v"][u2bank[t]] = S.n(svv)
                        V["silfree"][t] = S.n(svv)
                    V["t2free"] = S.n(smm)
                all_ev = S.n(svv)
                for t in range(QT):
                    for c in range(4):
                        S.op("vector", lambda e, t=t, c=c: e.reduce_max(
                            hmxp[t][:, c:c + 1], a16s(A_h + t * 8192 + c * 2048, 2048), AX.X,
                            apply_absolute_value=True),
                            waits=[(svv, all_ev)], inc=svv if c == 3 else None)
                    for c in range(8):
                        S.op("scalar", lambda e, t=t, c=c: e.activation(
                            a32s(B_dump, 1024), a16s(A_h + t * 8192 + c * 1024, 1024),
                            ACT.Square, accum_out=ssqp[t][:, c:c + 1]),
                            waits=[(svv, all_ev)] if c == 0 else [], inc=sss if c == 7 else None)
                hstat_v = S.n(svv)
                hstat_s = S.n(sss)

                for t in range(QT):
                    S.op("vector", lambda e, t=t: e.reduce_max(tA[0], hmxp[t][:, 0:4], AX.X),
                         waits=[(svv, hstat_v), (sss, hstat_s)], drain=True)
                    S.op("vector", lambda e, t=t: e.reduce_sum(tA[1], ssqp[t][:, 0:8], AX.X), drain=True)
                    S.op("vector", lambda e, t=t: e.reciprocal(c1h[t], tA[0]), drain=True)
                    S.op("vector", lambda e, t=t: e.tensor_scalar(c1h[t], c1h[t], 127.0, None, ALU.mult), drain=True)
                    S.op("vector", lambda e, t=t: e.tensor_scalar(tA[2], tA[1], g2c[t], g2c[t], ALU.mult, ALU.mult), drain=True)
                    S.op("vector", lambda e: e.tensor_scalar(tA[2], tA[2], 1.0 / FF, None, ALU.mult), drain=True)
                    S.op("vector", lambda e: e.tensor_tensor(tA[2], tA[2], eps6, ALU.add), inc=svv, drain=True)
                    vst = S.n(svv)
                    S.op("scalar", lambda e: e.activation(tA[2], tA[2], ACT.Ln), waits=[(svv, vst)])
                    S.op("scalar", lambda e: e.activation(tA[2], tA[2], ACT.Exp, scale=-0.5), inc=sss)
                    sst = S.n(sss)
                    S.op("vector", lambda e, t=t: e.tensor_scalar(tA[3], tA[0], g2c[t], None, ALU.mult),
                         waits=[(sss, sst)], drain=True)
                    S.op("vector", lambda e: e.tensor_scalar(tA[3], tA[3], tA[2], None, ALU.mult), drain=True)
                    S.op("vector", lambda e: e.tensor_scalar(tA[3], tA[3], 1e-5, 0.5 / 127.0, ALU.max, ALU.mult), drain=True)
                    S.op("vector", lambda e, t=t: e.tensor_scalar(coef3[t], tA[3], rs_w[2], None, ALU.mult),
                         inc=svv, drain=True)
                v_hstats = S.n(svv)

                prev_tq = 0
                for t in range(QT):
                    for fg in range(4):
                        hs = a16s(A_h + t * 8192 + fg * 2048, 2048)
                        vq = a32s(B_vq, 2048)
                        nq = a16s(A_nscr + (fg % 2) * 2048, 2048)
                        S.op("vector", lambda e, hs=hs, vq=vq, t=t:
                             e.tensor_scalar(vq, hs, c1h[t], C_ap, ALU.mult, ALU.add),
                             waits=[(svv, v_hstats)], drain=True)
                        S.op("vector", lambda e, vq=vq, nq=nq:
                             e.tensor_scalar(nq, vq, C_ap, None, ALU.subtract), inc=svv, drain=True)
                        v_nq = S.n(svv)
                        for b in range(16):
                            S.op("tensor", lambda e, nq=nq, b=b:
                                 e.transpose(psT[:, b * 128:(b + 1) * 128], nq[:, b * 128:(b + 1) * 128], ident[:]),
                                 waits=[(svv, v_nq)] if b == 0 else [],
                                 inc=smm if b == 15 else None)
                        prev_tq = S.n(smm)
                        S.op("vector", lambda e, hs=hs: e.tensor_copy(hs, psT[:, 0:2048]),
                             waits=[(smm, prev_tq)], inc=svv)
                v_b0 = S.n(svv)

                for mq in range(4):
                    mmfc = {}
                    for fg in range(16):
                        par = fg % 2
                        t3b = a16s(A_w1 + par * 2048, 2048)
                        t3g = t3s.rearrange("(a p) m -> p a m", p=128)
                        S.op("sync", lambda e, fg=fg, mq=mq, par=par, t3g=t3g:
                             e.dma_start(out=a16s(A_w1 + par * 2048, 2048).rearrange("p (a m) -> p a m", m=512),
                                         in_=t3g[:, fg * 4:(fg + 1) * 4, mq * 512:(mq + 1) * 512]),
                             waits=[(smm, mmfc.get(fg - 2, t3last[par])), (svv, v_b0), (dG, g_t3)],
                             inc=dW, amt=16)
                        t3d = S.n(dW)
                        for b in range(4):
                            fc = fg * 4 + b
                            for t in range(QT):
                                lhs = a16s(A_h + t * 8192 + fc * 128, 128)
                                S.op("tensor", lambda e, lhs=lhs, t3b=t3b, b=b, t=t, fc=fc:
                                     e.matmul(ps1[:, t * 512:(t + 1) * 512], lhs, t3b[:, b * 512:(b + 1) * 512],
                                              start=(fc == 0), stop=(fc == NFC - 1), skip_group_check=True),
                                     waits=([(dW, t3d), (svv, V["b1ev"])] if (t == 0 and b == 0 and fg == 0) else
                                            ([(dW, t3d)] if (t == 0 and b == 0) else [])),
                                     inc=smm if (t == QT - 1 and b == 3) else None)
                        mmfc[fg] = S.n(smm)
                    mm_y = S.n(smm)
                    t3last = [mm_y] * 4
                    for t in range(QT):
                        tti = q * QT + t
                        rx = a32s(B_rx + (t % 2) * 512, 512)
                        S.op("scalar", lambda e, xsrc=xsrc, tti=tti, mq=mq, rx=rx:
                             e.dma_start(out=rx, in_=xsrc[tti * 128:(tti + 1) * 128, mq * 512:(mq + 1) * 512]),
                             waits=[(svv, V["rx%d" % (t % 2)]), (dO, V["rxst%d" % (t % 2)])], inc=dX, amt=16)
                        rxd = S.n(dX)
                        S.op("vector", lambda e, rx=rx, t=t:
                             e.scalar_tensor_tensor(rx, ps1[:, t * 512:(t + 1) * 512], coef3[t], rx, ALU.mult, ALU.add),
                             waits=[(smm, mm_y), (dX, rxd)], inc=svv)
                        V["rx%d" % (t % 2)] = S.n(svv)
                        v_rx = S.n(svv)
                        S.op("scalar", lambda e, ydst=ydst, tti=tti, mq=mq, rx=rx:
                             e.dma_start(out=ydst[tti * 128:(tti + 1) * 128, mq * 512:(mq + 1) * 512], in_=rx),
                             waits=[(svv, v_rx)], inc=dO, amt=16)
                        V["rxst%d" % (t % 2)] = S.n(dO)
                    V["b1ev"] = S.n(svv)
                V["t1free"] = S.n(smm)
                V["t2free"] = S.n(smm)
                V["bfree_v"] = [S.n(svv)] * 6
                V["bfree_s"] = [S.n(sss)] * 6

        ffn_pass(x_in, x1s, "f1")
        ffn1_done_dO = S.n(dO)

        p3 = {"wqkv_free": S.n(smm), "ln0": S.n(smm), "ln1": S.n(smm),
              "wo_free": S.n(svv), "proj_ev": S.n(svv), "x1f": S.n(svv),
              "rev": S.n(svv), "pn": S.n(svv), "pt": S.n(svv), "oev": S.n(svv),
              "exp": S.n(sss), "xprep": S.n(svv)}

        prep_mm = S.n(smm)
        prep_free = {0: S.n(svv), 1: S.n(svv)}
        prep_ta = S.n(svv)
        for tti in range(NT):
            xbuf = a32s([B_x, B_vq][tti % 2], 2048)
            S.op("scalar", lambda e, tti=tti, xbuf=xbuf:
                 e.dma_start(out=xbuf, in_=x1s[tti * 128:(tti + 1) * 128, :]),
                 waits=[(svv, prep_free[tti % 2]), (dO, ffn1_done_dO)], inc=dX, amt=16)
            xd = S.n(dX)
            S.op("vector", lambda e, xbuf=xbuf: e.reduce_sum(tA[0], xbuf, AX.X),
                 waits=[(dX, xd)], drain=True)
            S.op("scalar", lambda e, xbuf=xbuf: e.activation(a32s(B_dump, 1024), xbuf[:, 0:1024],
                 ACT.Square, accum_out=tA[5]), waits=[(dX, xd), (svv, prep_ta)])
            S.op("scalar", lambda e, xbuf=xbuf: e.activation(a32s(B_dump, 1024), xbuf[:, 1024:2048],
                 ACT.Square, accum_out=tA[6]), inc=sss)
            sq0 = S.n(sss)
            S.op("vector", lambda e: e.tensor_scalar(tA[0], tA[0], 1.0 / M, None, ALU.mult),
                 waits=[(sss, sq0)], drain=True)
            S.op("vector", lambda e: e.tensor_tensor(tA[5], tA[5], tA[6], ALU.add), drain=True)
            S.op("vector", lambda e: e.tensor_scalar(tA[5], tA[5], 1.0 / M, None, ALU.mult), drain=True)
            S.op("vector", lambda e: e.tensor_scalar(tA[1], tA[0], tA[0], None, ALU.mult), drain=True)
            S.op("vector", lambda e: e.scalar_tensor_tensor(tA[1], tA[1], -1.0, tA[5], ALU.mult, ALU.add),
                 inc=svv, drain=True)
            prep_ta = S.n(svv)
            S.op("vector", lambda e: e.tensor_tensor(tA[4], tA[1], eps5, ALU.add), inc=svv, drain=True)
            vst = S.n(svv)
            S.op("scalar", lambda e: e.activation(tA[4], tA[4], ACT.Ln), waits=[(svv, vst)])
            S.op("scalar", lambda e: e.activation(invc, tA[4], ACT.Exp, scale=-0.5), inc=sss)
            sst = S.n(sss)
            S.op("vector", lambda e: e.tensor_scalar(c2a, tA[0], invc, -1.0, ALU.mult, ALU.mult),
                 waits=[(sss, sst)], drain=True)
            lt = a16s(P_lt, 2048)
            S.op("vector", lambda e, xbuf=xbuf, lt=lt: e.tensor_scalar(lt, xbuf, invc, c2a, ALU.mult, ALU.add),
                 waits=[(smm, prep_mm)], inc=svv, drain=True)
            prep_free[tti % 2] = S.n(svv)
            v_ln = S.n(svv)
            for mc in range(16):
                S.op("tensor", lambda e, lt=lt, mc=mc:
                     e.transpose(psT[:, mc * 128:(mc + 1) * 128], lt[:, mc * 128:(mc + 1) * 128], ident[:]),
                     waits=[(svv, v_ln)] if mc == 0 else [],
                     inc=smm if mc == 15 else None)
            prep_mm = S.n(smm)
            lt2 = a16s(P_lt + 2048, 2048)
            S.op("vector", lambda e, lt2=lt2: e.tensor_copy(lt2, psT[:, 0:2048]),
                 waits=[(smm, prep_mm), (dX, p3.get("lt2st", 0))], inc=svv)
            v_lt2 = S.n(svv)
            p3["xprep"] = S.n(svv)
            S.op("scalar", lambda e, lt2=lt2, tti=tti:
                 e.dma_start(out=ln2v[:, :, tti * 128:(tti + 1) * 128],
                             in_=lt2.rearrange("p (a t) -> p a t", t=128)),
                 waits=[(svv, v_lt2)], inc=dX, amt=16)
            p3["lt2st"] = S.n(dX)
        p3["x1f"] = S.n(svv)
        ln2_done = S.n(dX)

        for g in range(NG):
            src = [x1s, x1s2][g % 2]
            dst = [x1s2, x1s][g % 2]
            for i in range(GH):
                hd = g * GH + i
                S.op("gpsimd", lambda e, hd=hd, i=i:
                     e.dma_start(out=a16s(P_wo + i * 2048, 2048), in_=wo_in[hd]),
                     waits=[(svv, p3["wo_free"]), (smm, p3["wqkv_free"])], inc=dG, amt=16)
            wo_d = S.n(dG)
            for pr in range(2):
                for j in range(2):
                    hd = g * GH + pr * 2 + j
                    for wi, wt in enumerate([wq_in, wk_in, wv_in]):
                        dstsb = a16s(P_wqkv + (j * 3 + wi) * 2048, 2048)
                        S.op("gpsimd", lambda e, wt=wt, hd=hd, d=dstsb:
                             e.dma_start(out=d.rearrange("p (a d) -> p a d", d=128),
                                         in_=wt[:, hd, :].rearrange("(a p) d -> p a d", p=128)),
                             waits=[(smm, p3["wqkv_free"])], inc=dG, amt=16)
                wqkv_d = S.n(dG)
                for tc in range(4):
                    lnb = a16s(P_ln2c + (tc % 2) * 8192, 8192)
                    S.op("gpsimd", lambda e, tc=tc, lnb=lnb:
                         e.dma_start(out=lnb.rearrange("p (a t) -> p a t", t=512),
                                     in_=ln2v[:, :, tc * 512:(tc + 1) * 512]),
                         waits=[(dX, ln2_done), (smm, p3["ln%d" % (tc % 2)])], inc=dG, amt=16)
                    ln_d = S.n(dG)
                    for j in range(2):
                        i = pr * 2 + j
                        for wi in range(3):
                            wsb = a16s(P_wqkv + (j * 3 + wi) * 2048, 2048)
                            pdst = ps2[:, 0:512]
                            for mc in range(16):
                                S.op("tensor", lambda e, wsb=wsb, mc=mc, lnb=lnb, pdst=pdst:
                                     e.matmul(pdst, wsb[:, mc * 128:(mc + 1) * 128],
                                              lnb[:, mc * 512:(mc + 1) * 512],
                                              start=(mc == 0), stop=(mc == 15), skip_group_check=True),
                                     waits=([(dG, max(ln_d, wqkv_d)), (svv, p3["proj_ev"])] if mc == 0 else []),
                                     inc=smm if mc == 15 else None)
                            mm_p = S.n(smm)
                            if wi < 2:
                                kf = a32s(B_kq, 512)
                                S.op("vector", lambda e, kf=kf, pdst=pdst: e.tensor_copy(kf, pdst),
                                     waits=[(smm, mm_p)], inc=svv)
                                v_kf = S.n(svv)
                                S.op("tensor", lambda e, kf=kf: e.matmul(ps2[:, 512:1024], perm[:], kf,
                                     start=True, stop=True, skip_group_check=True),
                                     waits=[(svv, v_kf)], inc=smm)
                                mm_pk = S.n(smm)
                                S.op("vector", lambda e, kf=kf, tc=tc:
                                     e.tensor_tensor(kf, kf, cc_t[:, tc * 512:(tc + 1) * 512], ALU.mult),
                                     waits=[(smm, mm_pk)], drain=True)
                                S.op("vector", lambda e, tc=tc:
                                     e.tensor_tensor(a32s(B_kq + 512, 512), ps2[:, 512:1024],
                                                     ss_t[:, tc * 512:(tc + 1) * 512], ALU.mult), drain=True)
                                dsts = a16s([P_Q, P_K][wi] + i * 2048 + tc * 512, 512)
                                S.op("vector", lambda e, kf=kf, d=dsts:
                                     e.tensor_tensor(d, kf, a32s(B_kq + 512, 512), ALU.add), inc=svv, drain=True)
                                p3["proj_ev"] = S.n(svv)
                            else:
                                vt = a16s(P_vt, 512)
                                S.op("vector", lambda e, vt=vt, pdst=pdst: e.tensor_copy(vt, pdst),
                                     waits=[(smm, mm_p)], inc=svv)
                                v_vt = S.n(svv)
                                for b in range(4):
                                    S.op("tensor", lambda e, vt=vt, b=b:
                                         e.transpose(psT[:, b * 128:(b + 1) * 128], vt[:, b * 128:(b + 1) * 128], ident[:]),
                                         waits=[(svv, v_vt)] if b == 0 else [],
                                         inc=smm if b == 3 else None)
                                mm_vt = S.n(smm)
                                S.op("vector", lambda e, i=i, tc=tc:
                                     e.tensor_copy(a16s(P_V + i * 2048 + tc * 512, 512), psT[:, 0:512]),
                                     waits=[(smm, mm_vt)], inc=svv)
                                p3["proj_ev"] = S.n(svv)
                    p3["ln%d" % (tc % 2)] = S.n(smm)
                p3["wqkv_free"] = S.n(smm)
            proj_v = S.n(svv)

            for it in range(NT):
                jmax = (it + 1) * 128
                x1b = a32s(B_x1, 2048)
                S.op("scalar", lambda e, it=it, src=src, x1b=x1b:
                     e.dma_start(out=x1b, in_=src[it * 128:(it + 1) * 128, :]),
                     waits=[(svv, p3["x1f"]), (dX, p3.get("x1st", 0))], inc=dX, amt=16)
                x1d = S.n(dX)

                def plan_S(i, it=it, jmax=jmax):
                    nj = (jmax + 511) // 512
                    for jc in range(nj):
                        w = min(512, jmax - jc * 512)
                        S.op("tensor", lambda e, i=i, it=it, jc=jc, w=w:
                             e.matmul(ps1[:, jc * 512:jc * 512 + w],
                                      a16s(P_Q + i * 2048 + it * 128, 128),
                                      a16s(P_K + i * 2048 + jc * 512, w),
                                      start=True, stop=True, skip_group_check=True),
                             waits=([(svv, max(proj_v, p3["rev"])), (sss, p3["exp"])] if jc == 0 else []),
                             inc=smm if jc == nj - 1 else None)
                    return S.n(smm)

                mm_s = plan_S(0)
                for i in range(GH):
                    S.op("vector", lambda e, it=it: e.tensor_tensor(
                        ps1[:, it * 128:(it + 1) * 128], ps1[:, it * 128:(it + 1) * 128], mask[:], ALU.add),
                        waits=[(smm, mm_s)], inc=svv)
                    v_mask = S.n(svv)
                    S.op("scalar", lambda e, jmax=jmax: e.activation(
                        a16s(P_Pa, jmax), ps1[:, 0:jmax], ACT.Exp, scale=2.0 ** -7, accum_out=den_c),
                        waits=[(svv, v_mask)], inc=sss)
                    p3["exp"] = S.n(sss)
                    s_exp = S.n(sss)
                    if i + 1 < GH:
                        mm_s = plan_S(i + 1)
                    S.op("vector", lambda e: e.reciprocal(rden, den_c), waits=[(sss, s_exp)], drain=True)
                    S.op("vector", lambda e, jmax=jmax: e.tensor_scalar(
                        a16s(P_Pb, jmax), a16s(P_Pa, jmax), rden, None, ALU.mult), inc=svv, drain=True)
                    v_pn = S.n(svv)
                    for jt in range(it + 1):
                        S.op("tensor", lambda e, jt=jt:
                             e.transpose(psT[:, jt * 128:(jt + 1) * 128],
                                         a16s(P_Pb + jt * 128, 128), ident[:]),
                             waits=([(svv, max(v_pn, p3["pt"]))] if jt == 0 else []),
                             inc=smm if jt == it else None)
                    mm_pt = S.n(smm)
                    S.op("vector", lambda e, jmax=jmax: e.tensor_copy(a16s(P_PT, jmax), psT[:, 0:jmax]),
                         waits=[(smm, mm_pt)], inc=svv)
                    p3["pt"] = S.n(svv)
                    v_pt = S.n(svv)
                    for jt in range(it + 1):
                        S.op("tensor", lambda e, i=i, jt=jt:
                             e.matmul(ps2[:, 512:640], a16s(P_V + i * 2048 + jt * 128, 128),
                                      a16s(P_PT + jt * 128, 128),
                                      start=(jt == 0), stop=(jt == it), skip_group_check=True),
                             waits=([(svv, max(v_pt, p3["oev"]))] if jt == 0 else []),
                             inc=smm if jt == it else None)
                    mm_pv = S.n(smm)
                    S.op("vector", lambda e, i=i: e.tensor_copy(a16s(P_O + i * 128, 128), ps2[:, 512:640]),
                         waits=[(smm, mm_pv)], inc=svv)
                    p3["oev"] = S.n(svv)
                o_ev = S.n(svv)
                for i in range(GH):
                    for c in range(4):
                        S.op("tensor", lambda e, i=i, c=c:
                             e.matmul(ps1[:, c * 512:(c + 1) * 512], a16s(P_O + i * 128, 128),
                                      a16s(P_wo + i * 2048 + c * 512, 512),
                                      start=(i == 0), stop=(i == GH - 1), skip_group_check=True),
                             waits=([(svv, o_ev), (dG, wo_d), (sss, p3["exp"])] if (i == 0 and c == 0) else []),
                             inc=smm if (i == GH - 1 and c == 3) else None)
                mm_r = S.n(smm)
                for hh in range(2):
                    S.op("vector", lambda e, hh=hh, x1b=x1b:
                         e.tensor_tensor(x1b[:, hh * 1024:(hh + 1) * 1024],
                                         ps1[:, hh * 1024:(hh + 1) * 1024],
                                         x1b[:, hh * 1024:(hh + 1) * 1024], ALU.add),
                         waits=[(smm, mm_r), (dX, x1d)], inc=svv)
                p3["rev"] = S.n(svv)
                p3["x1f"] = S.n(svv)
                v_r = S.n(svv)
                S.op("scalar", lambda e, it=it, dst=dst, x1b=x1b:
                     e.dma_start(out=dst[it * 128:(it + 1) * 128, :], in_=x1b),
                     waits=[(svv, v_r)], inc=dX, amt=16)
                p3["x1st"] = S.n(dX)
            p3["wo_free"] = S.n(svv)
            p3["wqkv_free"] = S.n(smm)

        attn_done = S.n(dX)
        S.op("scalar", lambda e: e.nop(), waits=[(dX, attn_done)])

        ffn_pass(x1s, out_d, "f2")

        S.op("sync", lambda e: e.nop(), waits=[(dO, S.n(dO))])

        @block.sync
        def _(sync):
            S.emit("sync", sync)

        @block.vector
        def _(vector):
            S.emit("vector", vector)

        @block.scalar
        def _(scalar):
            S.emit("scalar", scalar)

        @block.tensor
        def _(tensor):
            S.emit("tensor", tensor)

        @block.gpsimd
        def _(gpsimd):
            S.emit("gpsimd", gpsimd)

    return nc


def _host_consts():
    ident = np.eye(128, dtype=bf16)
    perm = np.zeros((128, 128), np.float32)
    perm[(np.arange(128) + 64) % 128, np.arange(128)] = 1.0
    i = np.arange(128)[:, None]
    j = np.arange(128)[None, :]
    mask = np.where(j > i, np.float32(-1e30) * np.float32(128.0), np.float32(0.0)).astype(np.float32)
    half = 64
    freqs = np.power(np.float32(10000.0), -(np.arange(half, dtype=np.float32) / np.float32(half))).astype(np.float32)
    pos = np.arange(T, dtype=np.float32)[:, None]
    rad = (pos * freqs[None, :]).astype(np.float32)
    cos = np.cos(rad).astype(np.float32)
    sin = np.sin(rad).astype(np.float32)
    cc = np.ascontiguousarray(np.concatenate([cos.T, cos.T], axis=0).astype(np.float32))
    ss = np.ascontiguousarray(np.concatenate([sin.T, sin.T], axis=0).astype(np.float32))
    return ident, perm, mask, cc, ss


_CACHE = {}


def kernel(L0_attn_x, L0_attn_wq, L0_attn_wk, L0_attn_wv, L0_attn_wo,
           L0_ffn_w1, L0_ffn_w2, L0_ffn_w3):
    if "nc" not in _CACHE:
        _CACHE["nc"] = build_nc()
    nc = _CACHE["nc"]
    ident, perm, mask, cc, ss = _host_consts()
    w1t = np.ascontiguousarray(np.asarray(L0_ffn_w1, np.float32).T)
    w2t = np.ascontiguousarray(np.asarray(L0_ffn_w2, np.float32).T)
    w3t = np.ascontiguousarray(np.asarray(L0_ffn_w3, np.float32).T)
    wq = np.ascontiguousarray(np.asarray(L0_attn_wq, np.float32))
    wk = np.ascontiguousarray(np.asarray(L0_attn_wk, np.float32))
    wv = np.ascontiguousarray(np.asarray(L0_attn_wv, np.float32))
    wo = np.ascontiguousarray(np.asarray(L0_attn_wo, np.float32))
    x = np.asarray(L0_attn_x, np.float32)
    in_maps = []
    for c in range(8):
        in_maps.append({
            "x": np.ascontiguousarray(x[c]),
            "wq": wq, "wk": wk, "wv": wv, "wo": wo,
            "w1t": w1t, "w2t": w2t, "w3t": w3t,
            "ident": ident, "perm": perm, "mask": mask, "cch": cc, "ssh": ss,
        })
    res = run_bass_kernel_spmd(nc, in_maps, core_ids=list(range(8)))
    return np.stack([r["out"] for r in res.results]).astype(np.float32)


# version 24
# speedup vs baseline: 1.0177x; 1.0177x over previous
import contextlib
import numpy as np
import ml_dtypes
import concourse.bass as bass
import concourse.mybir as mybir
from concourse.bass_utils import run_bass_kernel_spmd

F32 = mybir.dt.float32
BF16 = mybir.dt.bfloat16
ALU = mybir.AluOpType
ACT = mybir.ActivationFunctionType
AX = mybir.AxisListType
bf16 = ml_dtypes.bfloat16

T = 2048
M = 2048
FF = 8192
H = 16
D = 128
NT = 16
NQ = 4
QT = 4
NFCC = 16
NFC = 64
NG = 4
GH = 4

MAGIC_K = 1.5 * 2.0 ** 82
MAGIC_C0 = 1.5 * 2.0 ** 23


class Sched:

    def __init__(self):
        self.ops = []
        self.counts = {}
        self.var = {}

    def n(self, sem):
        return self.counts.get(id(sem), 0)

    def op(self, eng, fn, waits=(), inc=None, amt=1, drain=False):
        incs = []
        if inc is not None:
            self.counts[id(inc)] = self.n(inc) + amt
            incs = [(inc, amt)]
        ws = tuple((s, v) for (s, v) in waits if v > 0)
        self.ops.append((eng, fn, ws, incs, drain))

    def emit(self, name, eng):
        seen = {}
        for (en, fn, waits, incs, drain) in self.ops:
            if en != name:
                continue
            for (sem, v) in waits:
                if seen.get(id(sem), 0) < v:
                    eng.wait_ge(sem, v)
                    seen[id(sem)] = v
            if drain:
                eng.drain()
            instr = fn(eng)
            for (sem, k) in incs:
                instr.then_inc(sem, k)


def build_nc():
    nc = bass.Bass("TRN2", target_bir_lowering=False, dynamic_dma_scratch_size=16384)

    def din(name, shape, dt):
        return nc.dram_tensor(name, shape, dt, kind="ExternalInput").ap()

    def dint(name, shape, dt):
        return nc.dram_tensor(name, shape, dt, kind="Internal").ap()

    x_in = din("x", [T, M], F32)
    wq_in = din("wq", [M, H, D], F32)
    wk_in = din("wk", [M, H, D], F32)
    wv_in = din("wv", [M, H, D], F32)
    wo_in = din("wo", [H, D, M], F32)
    w1t_in = din("w1t", [M, FF], F32)
    w2t_in = din("w2t", [M, FF], F32)
    w3t_in = din("w3t", [FF, M], F32)
    ident_in = din("ident", [128, 128], BF16)
    perm_in = din("perm", [128, 128], F32)
    mask_in = din("mask", [128, 128], F32)
    cch_in = din("cch", [128, T], F32)
    ssh_in = din("ssh", [128, T], F32)

    t1s = dint("t1s", [M, FF], BF16)
    t2s = dint("t2s", [M, FF], BF16)
    t3s = dint("t3s", [FF, M], BF16)
    x1s = dint("x1s", [T, M], F32)
    x1s2 = dint("x1s2", [T, M], F32)
    ln2s = dint("ln2s", [M, T], BF16)

    out_d = nc.dram_tensor("out", [T, M], F32, kind="ExternalOutput").ap()

    t1v = t1s.rearrange("(a p) f -> p a f", p=128)
    t2v = t2s.rearrange("(a p) f -> p a f", p=128)
    ln2v = ln2s.rearrange("(a p) t -> p a t", p=128)

    es = contextlib.ExitStack()
    with es:
        def sbt(name, cols, dt):
            return es.enter_context(nc.sbuf_tensor(name, [128, cols], dt))

        A16 = 65536
        A32 = 10240
        a16 = sbt("a16", A16, BF16)
        a32 = sbt("a32", A32, F32)
        cc_t = sbt("cc_t", T, F32)
        ss_t = sbt("ss_t", T, F32)
        ident = sbt("identsb", 128, BF16)
        perm = sbt("permsb", 128, F32)
        mask = sbt("masksb", 128, F32)
        stat = sbt("stat", 128, F32)
        part = sbt("part", 256, F32)
        ones = sbt("ones", 128, F32)

        ps1 = es.enter_context(nc.psum_tensor("ps1", [128, 2048], F32))
        ps2 = es.enter_context(nc.psum_tensor("ps2", [128, 1024], F32))
        psT = es.enter_context(nc.psum_tensor("psT", [128, 2048], BF16))

        dW = es.enter_context(nc.semaphore("dW"))
        dX = es.enter_context(nc.semaphore("dX"))
        dG = es.enter_context(nc.semaphore("dG"))
        dO = es.enter_context(nc.semaphore("dO"))
        smm = es.enter_context(nc.semaphore("smm"))
        svv = es.enter_context(nc.semaphore("svv"))
        sss = es.enter_context(nc.semaphore("sss"))
        sgg = es.enter_context(nc.semaphore("sgg"))

        block = es.enter_context(nc.Block())
        S = Sched()

        _si = [0]

        def stcol():
            i = _si[0]
            _si[0] += 1
            assert _si[0] <= 128
            return stat[:, i:i + 1]

        C_ap = stcol()
        flag = stcol()
        c127 = stcol()
        eps5 = stcol()
        eps6 = stcol()
        one_c = stcol()
        rs_w = [stcol() for _ in range(3)]
        sw_w = [stcol() for _ in range(3)]
        g1c = [stcol() for _ in range(QT)]
        g2c = [stcol() for _ in range(QT)]
        c1c = [stcol() for _ in range(QT)]
        c2c = [stcol() for _ in range(QT)]
        c1h = [stcol() for _ in range(QT)]
        coef3 = [stcol() for _ in range(QT)]
        invc = stcol()
        c2a = stcol()
        rden = stcol()
        pidx = stcol()
        jv = stcol()
        sgn_c = stcol()
        fr_c = stcol()
        tA = [stcol() for _ in range(8)]

        ssqp = [part[:, i * NFCC:(i + 1) * NFCC] for i in range(QT)]
        hmxp = [part[:, 64 + i * NFCC:64 + (i + 1) * NFCC] for i in range(QT)]
        wpart = part[:, 128:192]
        den_c = part[:, 192:193]

        A_h = 0
        A_nT = 32768
        A_w1 = 40960
        A_w2 = 57344
        A_nscr = 57344
        A_tern = 61440
        P_K = 0
        P_V = 8192
        P_Q = 16384
        P_wo = 24576
        P_wqkv = 32768
        P_ln2c = 45056
        P_Pa = [32768, 38912]
        P_Pb = [34816, 40960]
        P_PT = [36864, 43008]
        P_O = 61952
        P_vt = 62464
        P_lt = 61440

        B_x = 0
        B_vq = 2048
        B_dump = 4096
        B_sil = 5120
        B_rx = 6144
        B_kq = 2048
        B_x1 = 0

        def a16s(off, n):
            return a16[:, off:off + n]

        def a32s(off, n):
            return a32[:, off:off + n]

        S.op("sync", lambda e: e.dma_start(out=ident[:], in_=ident_in[:]), inc=dW, amt=16)
        S.op("sync", lambda e: e.dma_start(out=perm[:], in_=perm_in[:]), inc=dW, amt=16)
        S.op("sync", lambda e: e.dma_start(out=mask[:], in_=mask_in[:]), inc=dW, amt=16)
        S.op("sync", lambda e: e.dma_start(out=cc_t[:], in_=cch_in[:]), inc=dW, amt=16)
        S.op("sync", lambda e: e.dma_start(out=ss_t[:], in_=ssh_in[:]), inc=dW, amt=16)
        w0 = S.n(dW)

        S.op("vector", lambda e: e.memset(tA[0], 2.0 ** -30))
        S.op("vector", lambda e: e.tensor_scalar(tA[1], tA[0], 1.0, None, ALU.add), drain=True)
        S.op("vector", lambda e: e.tensor_scalar(tA[2], tA[1], -1.0, None, ALU.add), drain=True)
        S.op("vector", lambda e: e.tensor_scalar(C_ap, tA[2], MAGIC_K, MAGIC_C0, ALU.mult, ALU.add), drain=True)
        S.op("vector", lambda e: e.tensor_scalar(flag, tA[2], 2.0 ** 30, None, ALU.mult), drain=True)
        S.op("vector", lambda e: e.memset(c127, 127.0))
        S.op("vector", lambda e: e.memset(eps5, 1e-5))
        S.op("vector", lambda e: e.memset(eps6, 1e-6))
        S.op("vector", lambda e: e.memset(one_c, 1.0))
        S.op("vector", lambda e: e.memset(ones[:], 1.0), inc=svv)

        rope_rad = a32s(0, 2048)
        rope_t1 = a32s(2048, 2048)
        rope_t2 = a32s(4096, 2048)

        S.op("gpsimd", lambda e: e.iota(pidx, pattern=[[0, 1]], base=0, channel_multiplier=1,
                                        allow_small_or_imprecise_dtypes=True), inc=sgg)
        S.op("gpsimd", lambda e: e.iota(rope_rad[:], pattern=[[1, 2048]], base=0, channel_multiplier=0,
                                        allow_small_or_imprecise_dtypes=True), inc=sgg)
        g0 = S.n(sgg)
        S.op("vector", lambda e: e.tensor_scalar(jv, pidx, 64.0, None, ALU.is_ge),
             waits=[(sgg, g0)], drain=True)
        S.op("vector", lambda e: e.tensor_scalar(sgn_c, jv, 2.0, -1.0, ALU.mult, ALU.add), drain=True)
        S.op("vector", lambda e: e.scalar_tensor_tensor(jv, jv, -64.0, pidx, ALU.mult, ALU.add),
             inc=svv, drain=True)
        v_j = S.n(svv)
        S.op("scalar", lambda e: e.activation(tA[0], one_c, ACT.Ln, scale=10000.0), waits=[(svv, v_j)], inc=sss)
        s_ln = S.n(sss)
        S.op("vector", lambda e: e.tensor_scalar(tA[0], tA[0], -1.0 / 64.0, None, ALU.mult),
             waits=[(sss, s_ln)], inc=svv, drain=True)
        S.op("scalar", lambda e: e.activation(fr_c, jv, ACT.Exp, scale=tA[0]),
             waits=[(svv, S.n(svv))], inc=sss)
        s_fr = S.n(sss)
        S.op("scalar", lambda e: e.activation(tA[4], one_c, ACT.Arctan), inc=sss)
        s_at = S.n(sss)
        S.op("vector", lambda e: e.tensor_scalar(tA[4], tA[4], 4.0, None, ALU.mult), waits=[(sss, s_at)], drain=True)
        S.op("vector", lambda e: e.tensor_scalar(tA[3], tA[4], -1.0, None, ALU.mult), drain=True)
        S.op("vector", lambda e: e.tensor_scalar(tA[6], tA[4], 0.5, None, ALU.mult), drain=True)
        S.op("vector", lambda e: e.tensor_scalar(tA[7], tA[4], -2.0, None, ALU.mult), drain=True)
        S.op("vector", lambda e: e.tensor_scalar(tA[5], tA[4], 2.0, None, ALU.mult), drain=True)
        S.op("vector", lambda e: e.reciprocal(tA[5], tA[5]), drain=True)
        S.op("vector", lambda e: e.tensor_scalar(rope_rad[:], rope_rad[:], fr_c, None, ALU.mult),
             waits=[(sss, s_fr)], inc=svv, drain=True)

        def rope_dev(tbl, phase_half_pi):
            if phase_half_pi:
                S.op("vector", lambda e: e.tensor_scalar(rope_t1[:], rope_rad[:], tA[6], None, ALU.add), drain=True)
            else:
                S.op("vector", lambda e: e.tensor_copy(rope_t1[:], rope_rad[:]), drain=True)
            S.op("vector", lambda e: e.tensor_scalar(rope_t2[:], rope_t1[:], tA[5], None, ALU.mult), drain=True)
            S.op("vector", lambda e: e.tensor_scalar(rope_t2[:], rope_t2[:], C_ap, C_ap, ALU.add, ALU.subtract), drain=True)
            S.op("vector", lambda e: e.tensor_scalar(rope_t2[:], rope_t2[:], tA[7], None, ALU.mult), drain=True)
            S.op("vector", lambda e: e.tensor_tensor(rope_t2[:], rope_t2[:], rope_t1[:], ALU.add), drain=True)
            S.op("vector", lambda e: e.tensor_scalar(rope_t2[:], rope_t2[:], tA[4], None, ALU.min), drain=True)
            S.op("vector", lambda e: e.tensor_scalar(rope_t2[:], rope_t2[:], tA[3], None, ALU.max),
                 inc=svv, drain=True)
            vv0 = S.n(svv)
            S.op("scalar", lambda e: e.activation(rope_t1[:], rope_t2[:], ACT.Sin), waits=[(svv, vv0)], inc=sss)
            ss0 = S.n(sss)
            S.op("vector", lambda e, t=tbl: e.tensor_tensor(rope_t2[:], rope_t1[:], t, ALU.subtract),
                 waits=[(sss, ss0), (dW, w0)], drain=True)
            S.op("vector", lambda e, t=tbl: e.scalar_tensor_tensor(rope_t1[:], rope_t2[:], flag, t, ALU.mult, ALU.add),
                 drain=True)
            S.op("vector", lambda e, t=tbl: e.tensor_copy(t, rope_t1[:]), inc=svv, drain=True)

        rope_dev(ss_t[:], False)
        rope_dev(cc_t[:], True)
        S.op("vector", lambda e: e.tensor_scalar(ss_t[:], ss_t[:], sgn_c, None, ALU.mult), inc=svv, drain=True)
        v_rope = S.n(svv)

        wsrc = [w1t_in, w2t_in, w3t_in]
        tdst = [t1s, t2s, t3s]
        NCH = 32

        def chunk_ap(t, idx, wi):
            cols = FF if wi < 2 else M
            ccw = min(4096, cols)
            per = cols // ccw
            r, c = idx // per, idx % per
            return t[r * 128:(r + 1) * 128, c * ccw:(c + 1) * ccw]

        sc_state = {}

        def plan_scale_chunk(wi, idx, lastuse, loads):
            par = idx % 2
            buf = a32s(par * 4096, 4096)
            ccw = 4096 if wi < 2 else 2048
            buf = a32s(par * 4096, ccw)
            S.op("sync", lambda e, wi=wi, idx=idx, buf=buf:
                 e.dma_start(out=buf, in_=chunk_ap(wsrc[wi], idx, wi)),
                 waits=[(svv, lastuse[par])], inc=dW, amt=16)
            wd = S.n(dW)
            col = idx % 64
            S.op("vector", lambda e, buf=buf, col=col: e.reduce_sum(
                wpart[:, col:col + 1], buf, AX.X, apply_absolute_value=True),
                waits=[(dW, wd)], inc=svv)
            lastuse[par] = S.n(svv)

        def plan_scale_final(wi):
            n = 32 if wi < 2 else 64
            S.op("vector", lambda e, n=n: e.reduce_sum(tA[1], wpart[:, 0:n], AX.X), inc=svv, drain=True)
            vsum = S.n(svv)
            S.op("tensor", lambda e: e.matmul(ps2[:, 0:1], ones[:], tA[1], start=True, stop=True),
                 waits=[(svv, vsum)], inc=smm)
            mm0 = S.n(smm)
            S.op("vector", lambda e, wi=wi: e.tensor_scalar(rs_w[wi], ps2[:, 0:1], 1.0 / 16777216.0, None, ALU.mult),
                 waits=[(smm, mm0)], drain=True)
            S.op("vector", lambda e, wi=wi: e.tensor_scalar(rs_w[wi], rs_w[wi], 1e-5, None, ALU.max), drain=True)
            S.op("vector", lambda e, wi=wi: e.reciprocal(sw_w[wi], rs_w[wi]), inc=svv, drain=True)

        def plan_tern_chunk(wi, idx, lastuse, laststore):
            par = idx % 2
            fbuf = a32s(8192 + par * 1024, 1024)
            tbuf = a16s(A_tern + par * 1024, 1024)
            rows, cols = (M, FF) if wi < 2 else (FF, M)
            per = cols // 1024
            r, c = idx // per, idx % per
            S.op("sync", lambda e, wi=wi, r=r, c=c, fbuf=fbuf:
                 e.dma_start(out=fbuf, in_=wsrc[wi][r * 128:(r + 1) * 128, c * 1024:(c + 1) * 1024]),
                 waits=[(svv, lastuse[par]), (dG, laststore[par])], inc=dW, amt=16)
            wd = S.n(dW)
            S.op("vector", lambda e, fbuf=fbuf, wi=wi:
                 e.tensor_scalar(fbuf, fbuf, sw_w[wi], C_ap, ALU.mult, ALU.add),
                 waits=[(dW, wd)], drain=True)
            S.op("vector", lambda e, fbuf=fbuf: e.tensor_scalar(fbuf, fbuf, C_ap, None, ALU.subtract),
                 drain=True)
            S.op("vector", lambda e, fbuf=fbuf, tbuf=tbuf:
                 e.tensor_scalar(tbuf, fbuf, 1.0, -1.0, ALU.min, ALU.max), inc=svv, drain=True)
            vv = S.n(svv)
            lastuse[par] = vv
            S.op("gpsimd", lambda e, wi=wi, r=r, c=c, tbuf=tbuf:
                 e.dma_start(out=tdst[wi][r * 128:(r + 1) * 128, c * 1024:(c + 1) * 1024], in_=tbuf),
                 waits=[(svv, vv)], inc=dG, amt=16)
            laststore[par] = S.n(dG)

        sc_use = [max(S.n(svv), 1), max(S.n(svv), 1)]
        for idx in range(32):
            plan_scale_chunk(0, idx, sc_use, None)
        plan_scale_final(0)
        tn_use = [S.n(svv), S.n(svv)]
        tn_st = [0, 0]
        for wi in range(3):
            nsc = 32 if wi == 0 else 64
            step = 128 // nsc
            for idx in range(128):
                plan_tern_chunk(wi, idx, tn_use, tn_st)
                if wi < 2 and idx % step == 0 and idx // step < nsc:
                    plan_scale_chunk(wi + 1, idx // step, sc_use, None)
            if wi < 2:
                plan_scale_final(wi + 1)
            if wi == 0:
                g_t1 = S.n(dG)
            elif wi == 1:
                g_t2 = S.n(dG)
        g_tern = S.n(dG)
        g_t3 = g_tern

        def ffn_pass(xsrc, ydst, tag):
            V = {"xfree": S.n(svv),
                 "rx0": S.n(svv), "rx1": S.n(svv), "b1ev": S.n(svv),
                 "rxst0": S.n(dO), "rxst1": S.n(dO), "ring": 0,
                 "t1free": S.n(smm), "t2free": S.n(smm),
                 "bfree_v": [S.n(svv)] * 6, "bfree_s": [S.n(sss)] * 6,
                 "silfree": [S.n(svv)] * 4}
            t3last = [S.n(smm)] * 3
            for q in range(NQ):
                aprep_v = 0
                aprep_s = 0
                for t in range(QT):
                    tti = q * QT + t
                    xbuf = a32s(B_x, 2048)
                    S.op("scalar", lambda e, xsrc=xsrc, tti=tti, xbuf=xbuf:
                         e.dma_start(out=xbuf, in_=xsrc[tti * 128:(tti + 1) * 128, :]),
                         waits=[(svv, V["xfree"])], inc=dX, amt=16)
                    xd = S.n(dX)
                    S.op("vector", lambda e, xbuf=xbuf: e.reduce_sum(tA[0], xbuf, AX.X),
                         waits=[(dX, xd)], drain=True)
                    S.op("scalar", lambda e, xbuf=xbuf: e.activation(a32s(B_dump, 1024), xbuf[:, 0:1024],
                         ACT.Square, accum_out=tA[5]), waits=[(dX, xd)])
                    S.op("scalar", lambda e, xbuf=xbuf: e.activation(a32s(B_dump, 1024), xbuf[:, 1024:2048],
                         ACT.Square, accum_out=tA[6]), inc=sss)
                    sq0 = S.n(sss)
                    S.op("vector", lambda e, xbuf=xbuf: e.reduce_max(tA[2], xbuf, AX.X), drain=True)
                    S.op("vector", lambda e, xbuf=xbuf: e.tensor_reduce(tA[3], xbuf, AX.X, ALU.min), drain=True)
                    S.op("vector", lambda e: e.tensor_scalar(tA[0], tA[0], 1.0 / M, None, ALU.mult),
                         waits=[(sss, sq0)], drain=True)
                    S.op("vector", lambda e: e.tensor_tensor(tA[5], tA[5], tA[6], ALU.add), drain=True)
                    S.op("vector", lambda e: e.tensor_scalar(tA[5], tA[5], 1.0 / M, None, ALU.mult), drain=True)
                    S.op("vector", lambda e: e.tensor_scalar(tA[1], tA[0], tA[0], None, ALU.mult), drain=True)
                    S.op("vector", lambda e: e.scalar_tensor_tensor(tA[1], tA[1], -1.0, tA[5], ALU.mult, ALU.add), drain=True)
                    S.op("vector", lambda e: e.tensor_tensor(tA[2], tA[2], tA[0], ALU.subtract), drain=True)
                    S.op("vector", lambda e: e.tensor_tensor(tA[3], tA[0], tA[3], ALU.subtract), drain=True)
                    S.op("vector", lambda e: e.tensor_tensor(tA[2], tA[2], tA[3], ALU.max), drain=True)
                    S.op("vector", lambda e, t=t: e.reciprocal(c1c[t], tA[2]), drain=True)
                    S.op("vector", lambda e, t=t: e.tensor_scalar(c1c[t], c1c[t], 127.0, None, ALU.mult), drain=True)
                    S.op("vector", lambda e, t=t: e.tensor_scalar(c2c[t], tA[0], c1c[t], -1.0, ALU.mult, ALU.mult), drain=True)
                    S.op("vector", lambda e: e.tensor_tensor(tA[4], tA[1], eps5, ALU.add), inc=svv, drain=True)
                    vst = S.n(svv)
                    S.op("scalar", lambda e: e.activation(tA[4], tA[4], ACT.Ln), waits=[(svv, vst)])
                    S.op("scalar", lambda e: e.activation(tA[4], tA[4], ACT.Exp, scale=-0.5), inc=sss)
                    sst = S.n(sss)
                    S.op("vector", lambda e: e.tensor_scalar(tA[6], tA[4], tA[4], None, ALU.mult),
                         waits=[(sss, sst)], drain=True)
                    S.op("vector", lambda e: e.tensor_tensor(tA[6], tA[1], tA[6], ALU.mult), drain=True)
                    S.op("vector", lambda e: e.tensor_tensor(tA[6], tA[6], eps6, ALU.add), inc=svv, drain=True)
                    vst = S.n(svv)
                    S.op("scalar", lambda e: e.activation(tA[6], tA[6], ACT.Ln), waits=[(svv, vst)])
                    S.op("scalar", lambda e: e.activation(tA[6], tA[6], ACT.Exp, scale=-0.5), inc=sss)
                    sst = S.n(sss)
                    S.op("vector", lambda e: e.tensor_scalar(tA[7], tA[2], tA[4], None, ALU.mult),
                         waits=[(sss, sst)], drain=True)
                    S.op("vector", lambda e: e.tensor_scalar(tA[7], tA[7], tA[6], None, ALU.mult), drain=True)
                    S.op("vector", lambda e: e.tensor_scalar(tA[7], tA[7], 1e-5, 1.0 / 127.0, ALU.max, ALU.mult), drain=True)
                    S.op("vector", lambda e, t=t: e.tensor_scalar(g1c[t], tA[7], rs_w[0], None, ALU.mult), drain=True)
                    S.op("vector", lambda e, t=t: e.tensor_scalar(g2c[t], tA[7], rs_w[1], None, ALU.mult), inc=svv, drain=True)
                    for hh in range(2):
                        vq = a32s(B_vq + hh * 1024, 1024)
                        nq = a16s(A_nscr + (t % 2) * 2048 + hh * 1024, 1024)
                        S.op("vector", lambda e, xbuf=xbuf, vq=vq, t=t, hh=hh:
                             e.tensor_scalar(vq, xbuf[:, hh * 1024:(hh + 1) * 1024], c1c[t], c2c[t], ALU.mult, ALU.add),
                             drain=True)
                        S.op("vector", lambda e, vq=vq, nq=nq:
                             e.tensor_scalar(nq, vq, C_ap, C_ap, ALU.add, ALU.subtract), inc=svv, drain=True)
                    V["xfree"] = S.n(svv)
                    vq_done = S.n(svv)
                    nq_full = a16s(A_nscr + (t % 2) * 2048, 2048)
                    for mc in range(16):
                        S.op("tensor", lambda e, nq=nq_full, mc=mc:
                             e.transpose(psT[:, mc * 128:(mc + 1) * 128], nq[:, mc * 128:(mc + 1) * 128], ident[:]),
                             waits=[(svv, vq_done)] if mc == 0 else [],
                             inc=smm if mc == 15 else None)
                    tmm = S.n(smm)
                    nT_view = a16s(A_nT, 8192).rearrange("p (a t) -> p a t", t=512)
                    S.op("vector", lambda e, t=t, nv=nT_view: e.tensor_copy(
                        nv[:, :, t * 128:(t + 1) * 128], psT[:, 0:2048]),
                        waits=[(smm, tmm)], inc=svv)
                aprep_v = S.n(svv)
                aprep_s = S.n(sss)

                for fcc in range(NFCC):
                    t1b = a16s(A_w1, 8192)
                    t2b = a16s(A_w1 + 8192, 8192)
                    S.op("sync", lambda e, fcc=fcc:
                         e.dma_start(out=a16s(A_w1, 8192).rearrange("p (a f) -> p a f", f=512),
                                     in_=t1v[:, :, fcc * 512:(fcc + 1) * 512]),
                         waits=[(smm, V["t1free"]), (dG, g_t1)], inc=dW, amt=16)
                    w1d = S.n(dW)
                    S.op("sync", lambda e, fcc=fcc:
                         e.dma_start(out=a16s(A_w1 + 8192, 8192).rearrange("p (a f) -> p a f", f=512),
                                     in_=t2v[:, :, fcc * 512:(fcc + 1) * 512]),
                         waits=[(smm, V["t2free"]), (dG, g_t2)], inc=dW, amt=16)
                    w2d = S.n(dW)
                    def bank_ap(b):
                        return ps1[:, b * 512:(b + 1) * 512] if b < 4 else ps2[:, (b - 4) * 512:(b - 3) * 512]
                    u1bank = [(V["ring"] + i) % 6 for i in range(4)]
                    u2bank = [(V["ring"] + 4 + i) % 6 for i in range(4)]
                    V["ring"] = (V["ring"] + 8) % 6
                    mmu1 = [0] * 4
                    for t in range(QT):
                        b = u1bank[t]
                        pu1 = bank_ap(b)
                        for mc in range(16):
                            lhs = a16s(A_nT + mc * 512 + t * 128, 128)
                            S.op("tensor", lambda e, lhs=lhs, t1b=t1b, mc=mc, pu1=pu1:
                                 e.matmul(pu1, lhs, t1b[:, mc * 512:(mc + 1) * 512],
                                          start=(mc == 0), stop=(mc == 15), skip_group_check=True),
                                 waits=([(dW, w1d), (svv, max(aprep_v, V["bfree_v"][b])),
                                         (sss, V["bfree_s"][b])] if mc == 0 else []),
                                 inc=smm if mc == 15 else None)
                        mmu1[t] = S.n(smm)
                    V["t1free"] = S.n(smm)
                    for t in range(QT):
                        b = u2bank[t]
                        pu2 = bank_ap(b)
                        for mc in range(16):
                            lhs = a16s(A_nT + mc * 512 + t * 128, 128)
                            S.op("tensor", lambda e, lhs=lhs, t2b=t2b, mc=mc, pu2=pu2:
                                 e.matmul(pu2, lhs, t2b[:, mc * 512:(mc + 1) * 512],
                                          start=(mc == 0), stop=(mc == 15), skip_group_check=True),
                                 waits=([(dW, w2d), (svv, V["bfree_v"][b]), (sss, V["bfree_s"][b])]
                                        if mc == 0 else []),
                                 inc=smm if mc == 15 else None)
                        mm_u2 = S.n(smm)
                        silb = a32s([B_sil, B_sil + 512, 7168, 7680][t], 512)
                        pu1 = bank_ap(u1bank[t])
                        S.op("scalar", lambda e, pu1=pu1, silb=silb, t=t:
                             e.activation(silb, pu1, ACT.Silu, scale=g1c[t]),
                             waits=[(smm, mmu1[t]), (sss, aprep_s), (svv, V["silfree"][t])], inc=sss)
                        V["bfree_s"][u1bank[t]] = S.n(sss)
                        s_sil = S.n(sss)
                        hslice = a16s(A_h + t * 8192 + fcc * 512, 512)
                        S.op("vector", lambda e, silb=silb, pu2=pu2, hs=hslice:
                             e.tensor_tensor(hs, silb, pu2, ALU.mult),
                             waits=[(sss, s_sil), (smm, mm_u2)], inc=svv)
                        V["bfree_v"][u2bank[t]] = S.n(svv)
                        V["silfree"][t] = S.n(svv)
                    V["t2free"] = S.n(smm)
                all_ev = S.n(svv)
                for t in range(QT):
                    for c in range(4):
                        S.op("vector", lambda e, t=t, c=c: e.reduce_max(
                            hmxp[t][:, c:c + 1], a16s(A_h + t * 8192 + c * 2048, 2048), AX.X,
                            apply_absolute_value=True),
                            waits=[(svv, all_ev)], inc=svv if c == 3 else None)
                    for c in range(8):
                        S.op("scalar", lambda e, t=t, c=c: e.activation(
                            a32s(B_dump, 1024), a16s(A_h + t * 8192 + c * 1024, 1024),
                            ACT.Square, accum_out=ssqp[t][:, c:c + 1]),
                            waits=[(svv, all_ev)] if c == 0 else [], inc=sss if c == 7 else None)
                hstat_v = S.n(svv)
                hstat_s = S.n(sss)

                for t in range(QT):
                    S.op("vector", lambda e, t=t: e.reduce_max(tA[0], hmxp[t][:, 0:4], AX.X),
                         waits=[(svv, hstat_v), (sss, hstat_s)], drain=True)
                    S.op("vector", lambda e, t=t: e.reduce_sum(tA[1], ssqp[t][:, 0:8], AX.X), drain=True)
                    S.op("vector", lambda e, t=t: e.reciprocal(c1h[t], tA[0]), drain=True)
                    S.op("vector", lambda e, t=t: e.tensor_scalar(c1h[t], c1h[t], 127.0, None, ALU.mult), drain=True)
                    S.op("vector", lambda e, t=t: e.tensor_scalar(tA[2], tA[1], g2c[t], g2c[t], ALU.mult, ALU.mult), drain=True)
                    S.op("vector", lambda e: e.tensor_scalar(tA[2], tA[2], 1.0 / FF, None, ALU.mult), drain=True)
                    S.op("vector", lambda e: e.tensor_tensor(tA[2], tA[2], eps6, ALU.add), inc=svv, drain=True)
                    vst = S.n(svv)
                    S.op("scalar", lambda e: e.activation(tA[2], tA[2], ACT.Ln), waits=[(svv, vst)])
                    S.op("scalar", lambda e: e.activation(tA[2], tA[2], ACT.Exp, scale=-0.5), inc=sss)
                    sst = S.n(sss)
                    S.op("vector", lambda e, t=t: e.tensor_scalar(tA[3], tA[0], g2c[t], None, ALU.mult),
                         waits=[(sss, sst)], drain=True)
                    S.op("vector", lambda e: e.tensor_scalar(tA[3], tA[3], tA[2], None, ALU.mult), drain=True)
                    S.op("vector", lambda e: e.tensor_scalar(tA[3], tA[3], 1e-5, 0.5 / 127.0, ALU.max, ALU.mult), drain=True)
                    S.op("vector", lambda e, t=t: e.tensor_scalar(coef3[t], tA[3], rs_w[2], None, ALU.mult),
                         inc=svv, drain=True)
                v_hstats = S.n(svv)

                prev_tq = 0
                for t in range(QT):
                    for fg in range(4):
                        hs = a16s(A_h + t * 8192 + fg * 2048, 2048)
                        vq = a32s(B_vq, 2048)
                        nq = a16s(A_nscr + (fg % 2) * 2048, 2048)
                        S.op("vector", lambda e, hs=hs, vq=vq, t=t:
                             e.tensor_scalar(vq, hs, c1h[t], C_ap, ALU.mult, ALU.add),
                             waits=[(svv, v_hstats)], drain=True)
                        S.op("vector", lambda e, vq=vq, nq=nq:
                             e.tensor_scalar(nq, vq, C_ap, None, ALU.subtract), inc=svv, drain=True)
                        v_nq = S.n(svv)
                        for b in range(16):
                            S.op("tensor", lambda e, nq=nq, b=b:
                                 e.transpose(psT[:, b * 128:(b + 1) * 128], nq[:, b * 128:(b + 1) * 128], ident[:]),
                                 waits=[(svv, v_nq)] if b == 0 else [],
                                 inc=smm if b == 15 else None)
                        prev_tq = S.n(smm)
                        S.op("vector", lambda e, hs=hs: e.tensor_copy(hs, psT[:, 0:2048]),
                             waits=[(smm, prev_tq)], inc=svv)
                v_b0 = S.n(svv)

                for mq in range(4):
                    mmfc = {}
                    for fg in range(16):
                        par = fg % 3
                        t3b = a16s(A_w1 + par * 2048, 2048)
                        t3g = t3s.rearrange("(a p) m -> p a m", p=128)
                        S.op("sync", lambda e, fg=fg, mq=mq, par=par, t3g=t3g:
                             e.dma_start(out=a16s(A_w1 + par * 2048, 2048).rearrange("p (a m) -> p a m", m=512),
                                         in_=t3g[:, fg * 4:(fg + 1) * 4, mq * 512:(mq + 1) * 512]),
                             waits=[(smm, mmfc.get(fg - 3, t3last[par])), (svv, v_b0), (dG, g_t3)],
                             inc=dW, amt=16)
                        t3d = S.n(dW)
                        for b in range(4):
                            fc = fg * 4 + b
                            for t in range(QT):
                                lhs = a16s(A_h + t * 8192 + fc * 128, 128)
                                S.op("tensor", lambda e, lhs=lhs, t3b=t3b, b=b, t=t, fc=fc:
                                     e.matmul(ps1[:, t * 512:(t + 1) * 512], lhs, t3b[:, b * 512:(b + 1) * 512],
                                              start=(fc == 0), stop=(fc == NFC - 1), skip_group_check=True),
                                     waits=([(dW, t3d), (svv, V["b1ev"])] if (t == 0 and b == 0 and fg == 0) else
                                            ([(dW, t3d)] if (t == 0 and b == 0) else [])),
                                     inc=smm if (t == QT - 1 and b == 3) else None)
                        mmfc[fg] = S.n(smm)
                    mm_y = S.n(smm)
                    t3last = [mm_y] * 3
                    for t in range(QT):
                        tti = q * QT + t
                        rx = a32s(B_rx + (t % 2) * 512, 512)
                        S.op("scalar", lambda e, xsrc=xsrc, tti=tti, mq=mq, rx=rx:
                             e.dma_start(out=rx, in_=xsrc[tti * 128:(tti + 1) * 128, mq * 512:(mq + 1) * 512]),
                             waits=[(svv, V["rx%d" % (t % 2)]), (dO, V["rxst%d" % (t % 2)])], inc=dX, amt=16)
                        rxd = S.n(dX)
                        S.op("vector", lambda e, rx=rx, t=t:
                             e.scalar_tensor_tensor(rx, ps1[:, t * 512:(t + 1) * 512], coef3[t], rx, ALU.mult, ALU.add),
                             waits=[(smm, mm_y), (dX, rxd)], inc=svv)
                        V["rx%d" % (t % 2)] = S.n(svv)
                        v_rx = S.n(svv)
                        S.op("scalar", lambda e, ydst=ydst, tti=tti, mq=mq, rx=rx:
                             e.dma_start(out=ydst[tti * 128:(tti + 1) * 128, mq * 512:(mq + 1) * 512], in_=rx),
                             waits=[(svv, v_rx)], inc=dO, amt=16)
                        V["rxst%d" % (t % 2)] = S.n(dO)
                    V["b1ev"] = S.n(svv)
                V["t1free"] = S.n(smm)
                V["t2free"] = S.n(smm)
                V["bfree_v"] = [S.n(svv)] * 6
                V["bfree_s"] = [S.n(sss)] * 6

        ffn_pass(x_in, x1s, "f1")
        ffn1_done_dO = S.n(dO)

        p3 = {"wqkv_free": S.n(smm), "ln0": S.n(smm), "ln1": S.n(smm),
              "wo_free": S.n(svv), "proj_ev": S.n(svv), "x1f": S.n(svv),
              "rev": S.n(svv), "pn0": S.n(svv), "pn1": S.n(svv), "pt": S.n(svv), "oev": S.n(svv),
              "exp": S.n(sss), "xprep": S.n(svv)}

        prep_mm = S.n(smm)
        prep_free = {0: S.n(svv), 1: S.n(svv)}
        prep_ta = S.n(svv)
        for tti in range(NT):
            xbuf = a32s([B_x, B_vq][tti % 2], 2048)
            S.op("scalar", lambda e, tti=tti, xbuf=xbuf:
                 e.dma_start(out=xbuf, in_=x1s[tti * 128:(tti + 1) * 128, :]),
                 waits=[(svv, prep_free[tti % 2]), (dO, ffn1_done_dO)], inc=dX, amt=16)
            xd = S.n(dX)
            S.op("vector", lambda e, xbuf=xbuf: e.reduce_sum(tA[0], xbuf, AX.X),
                 waits=[(dX, xd)], drain=True)
            S.op("scalar", lambda e, xbuf=xbuf: e.activation(a32s(B_dump, 1024), xbuf[:, 0:1024],
                 ACT.Square, accum_out=tA[5]), waits=[(dX, xd), (svv, prep_ta)])
            S.op("scalar", lambda e, xbuf=xbuf: e.activation(a32s(B_dump, 1024), xbuf[:, 1024:2048],
                 ACT.Square, accum_out=tA[6]), inc=sss)
            sq0 = S.n(sss)
            S.op("vector", lambda e: e.tensor_scalar(tA[0], tA[0], 1.0 / M, None, ALU.mult),
                 waits=[(sss, sq0)], drain=True)
            S.op("vector", lambda e: e.tensor_tensor(tA[5], tA[5], tA[6], ALU.add), drain=True)
            S.op("vector", lambda e: e.tensor_scalar(tA[5], tA[5], 1.0 / M, None, ALU.mult), drain=True)
            S.op("vector", lambda e: e.tensor_scalar(tA[1], tA[0], tA[0], None, ALU.mult), drain=True)
            S.op("vector", lambda e: e.scalar_tensor_tensor(tA[1], tA[1], -1.0, tA[5], ALU.mult, ALU.add),
                 inc=svv, drain=True)
            prep_ta = S.n(svv)
            S.op("vector", lambda e: e.tensor_tensor(tA[4], tA[1], eps5, ALU.add), inc=svv, drain=True)
            vst = S.n(svv)
            S.op("scalar", lambda e: e.activation(tA[4], tA[4], ACT.Ln), waits=[(svv, vst)])
            S.op("scalar", lambda e: e.activation(invc, tA[4], ACT.Exp, scale=-0.5), inc=sss)
            sst = S.n(sss)
            S.op("vector", lambda e: e.tensor_scalar(c2a, tA[0], invc, -1.0, ALU.mult, ALU.mult),
                 waits=[(sss, sst)], drain=True)
            lt = a16s(P_lt, 2048)
            S.op("vector", lambda e, xbuf=xbuf, lt=lt: e.tensor_scalar(lt, xbuf, invc, c2a, ALU.mult, ALU.add),
                 waits=[(smm, prep_mm)], inc=svv, drain=True)
            prep_free[tti % 2] = S.n(svv)
            v_ln = S.n(svv)
            for mc in range(16):
                S.op("tensor", lambda e, lt=lt, mc=mc:
                     e.transpose(psT[:, mc * 128:(mc + 1) * 128], lt[:, mc * 128:(mc + 1) * 128], ident[:]),
                     waits=[(svv, v_ln)] if mc == 0 else [],
                     inc=smm if mc == 15 else None)
            prep_mm = S.n(smm)
            lt2 = a16s(P_lt + 2048, 2048)
            S.op("vector", lambda e, lt2=lt2: e.tensor_copy(lt2, psT[:, 0:2048]),
                 waits=[(smm, prep_mm), (dX, p3.get("lt2st", 0))], inc=svv)
            v_lt2 = S.n(svv)
            p3["xprep"] = S.n(svv)
            S.op("scalar", lambda e, lt2=lt2, tti=tti:
                 e.dma_start(out=ln2v[:, :, tti * 128:(tti + 1) * 128],
                             in_=lt2.rearrange("p (a t) -> p a t", t=128)),
                 waits=[(svv, v_lt2)], inc=dX, amt=16)
            p3["lt2st"] = S.n(dX)
        p3["x1f"] = S.n(svv)
        ln2_done = S.n(dX)

        for g in range(NG):
            src = [x1s, x1s2][g % 2]
            dst = [x1s2, x1s][g % 2]
            for i in range(GH):
                hd = g * GH + i
                S.op("gpsimd", lambda e, hd=hd, i=i:
                     e.dma_start(out=a16s(P_wo + i * 2048, 2048), in_=wo_in[hd]),
                     waits=[(svv, p3["wo_free"]), (smm, p3["wqkv_free"])], inc=dG, amt=16)
            wo_d = S.n(dG)
            for pr in range(2):
                for j in range(2):
                    hd = g * GH + pr * 2 + j
                    for wi, wt in enumerate([wq_in, wk_in, wv_in]):
                        dstsb = a16s(P_wqkv + (j * 3 + wi) * 2048, 2048)
                        S.op("gpsimd", lambda e, wt=wt, hd=hd, d=dstsb:
                             e.dma_start(out=d.rearrange("p (a d) -> p a d", d=128),
                                         in_=wt[:, hd, :].rearrange("(a p) d -> p a d", p=128)),
                             waits=[(smm, p3["wqkv_free"])], inc=dG, amt=16)
                wqkv_d = S.n(dG)
                for tc in range(4):
                    lnb = a16s(P_ln2c + (tc % 2) * 8192, 8192)
                    S.op("gpsimd", lambda e, tc=tc, lnb=lnb:
                         e.dma_start(out=lnb.rearrange("p (a t) -> p a t", t=512),
                                     in_=ln2v[:, :, tc * 512:(tc + 1) * 512]),
                         waits=[(dX, ln2_done), (smm, p3["ln%d" % (tc % 2)])], inc=dG, amt=16)
                    ln_d = S.n(dG)
                    for j in range(2):
                        i = pr * 2 + j
                        for wi in range(3):
                            wsb = a16s(P_wqkv + (j * 3 + wi) * 2048, 2048)
                            pdst = ps2[:, 0:512]
                            for mc in range(16):
                                S.op("tensor", lambda e, wsb=wsb, mc=mc, lnb=lnb, pdst=pdst:
                                     e.matmul(pdst, wsb[:, mc * 128:(mc + 1) * 128],
                                              lnb[:, mc * 512:(mc + 1) * 512],
                                              start=(mc == 0), stop=(mc == 15), skip_group_check=True),
                                     waits=([(dG, max(ln_d, wqkv_d)), (svv, p3["proj_ev"])] if mc == 0 else []),
                                     inc=smm if mc == 15 else None)
                            mm_p = S.n(smm)
                            if wi < 2:
                                kf = a32s(B_kq, 512)
                                S.op("vector", lambda e, kf=kf, pdst=pdst: e.tensor_copy(kf, pdst),
                                     waits=[(smm, mm_p)], inc=svv)
                                v_kf = S.n(svv)
                                S.op("tensor", lambda e, kf=kf: e.matmul(ps2[:, 512:1024], perm[:], kf,
                                     start=True, stop=True, skip_group_check=True),
                                     waits=[(svv, v_kf)], inc=smm)
                                mm_pk = S.n(smm)
                                S.op("vector", lambda e, kf=kf, tc=tc:
                                     e.tensor_tensor(kf, kf, cc_t[:, tc * 512:(tc + 1) * 512], ALU.mult),
                                     waits=[(smm, mm_pk)], drain=True)
                                S.op("vector", lambda e, tc=tc:
                                     e.tensor_tensor(a32s(B_kq + 512, 512), ps2[:, 512:1024],
                                                     ss_t[:, tc * 512:(tc + 1) * 512], ALU.mult), drain=True)
                                dsts = a16s([P_Q, P_K][wi] + i * 2048 + tc * 512, 512)
                                S.op("vector", lambda e, kf=kf, d=dsts:
                                     e.tensor_tensor(d, kf, a32s(B_kq + 512, 512), ALU.add), inc=svv, drain=True)
                                p3["proj_ev"] = S.n(svv)
                            else:
                                vt = a16s(P_vt, 512)
                                S.op("vector", lambda e, vt=vt, pdst=pdst: e.tensor_copy(vt, pdst),
                                     waits=[(smm, mm_p)], inc=svv)
                                v_vt = S.n(svv)
                                for b in range(4):
                                    S.op("tensor", lambda e, vt=vt, b=b:
                                         e.transpose(psT[:, b * 128:(b + 1) * 128], vt[:, b * 128:(b + 1) * 128], ident[:]),
                                         waits=[(svv, v_vt)] if b == 0 else [],
                                         inc=smm if b == 3 else None)
                                mm_vt = S.n(smm)
                                S.op("vector", lambda e, i=i, tc=tc:
                                     e.tensor_copy(a16s(P_V + i * 2048 + tc * 512, 512), psT[:, 0:512]),
                                     waits=[(smm, mm_vt)], inc=svv)
                                p3["proj_ev"] = S.n(svv)
                    p3["ln%d" % (tc % 2)] = S.n(smm)
                p3["wqkv_free"] = S.n(smm)
            proj_v = S.n(svv)

            for it in range(NT):
                jmax = (it + 1) * 128
                x1b = a32s(B_x1, 2048)
                S.op("scalar", lambda e, it=it, src=src, x1b=x1b:
                     e.dma_start(out=x1b, in_=src[it * 128:(it + 1) * 128, :]),
                     waits=[(svv, p3["x1f"]), (dX, p3.get("x1st", 0))], inc=dX, amt=16)
                x1d = S.n(dX)

                def plan_S(i, it=it, jmax=jmax):
                    nj = (jmax + 511) // 512
                    for jc in range(nj):
                        w = min(512, jmax - jc * 512)
                        S.op("tensor", lambda e, i=i, it=it, jc=jc, w=w:
                             e.matmul(ps1[:, jc * 512:jc * 512 + w],
                                      a16s(P_Q + i * 2048 + it * 128, 128),
                                      a16s(P_K + i * 2048 + jc * 512, w),
                                      start=True, stop=True, skip_group_check=True),
                             waits=([(svv, max(proj_v, p3["rev"])), (sss, p3["exp"])] if jc == 0 else []),
                             inc=smm if jc == nj - 1 else None)
                    return S.n(smm)

                mm_s = plan_S(0)
                for i in range(GH):
                    pp = i % 2
                    Pa, Pb, PT = P_Pa[pp], P_Pb[pp], P_PT[pp]
                    S.op("vector", lambda e, it=it: e.tensor_tensor(
                        ps1[:, it * 128:(it + 1) * 128], ps1[:, it * 128:(it + 1) * 128], mask[:], ALU.add),
                        waits=[(smm, mm_s)], inc=svv)
                    v_mask = S.n(svv)
                    S.op("scalar", lambda e, jmax=jmax, Pa=Pa: e.activation(
                        a16s(Pa, jmax), ps1[:, 0:jmax], ACT.Exp, scale=2.0 ** -7, accum_out=den_c),
                        waits=[(svv, v_mask), (svv, p3["pn%d" % pp])], inc=sss)
                    p3["exp"] = S.n(sss)
                    s_exp = S.n(sss)
                    if i + 1 < GH:
                        mm_s = plan_S(i + 1)
                    S.op("vector", lambda e: e.reciprocal(rden, den_c), waits=[(sss, s_exp)], drain=True)
                    S.op("vector", lambda e, jmax=jmax, Pa=Pa, Pb=Pb: e.tensor_scalar(
                        a16s(Pb, jmax), a16s(Pa, jmax), rden, None, ALU.mult), inc=svv, drain=True)
                    p3["pn%d" % pp] = S.n(svv)
                    v_pn = S.n(svv)
                    for jt in range(it + 1):
                        S.op("tensor", lambda e, jt=jt, Pb=Pb:
                             e.transpose(psT[:, jt * 128:(jt + 1) * 128],
                                         a16s(Pb + jt * 128, 128), ident[:]),
                             waits=([(svv, max(v_pn, p3["pt"]))] if jt == 0 else []),
                             inc=smm if jt == it else None)
                    mm_pt = S.n(smm)
                    S.op("vector", lambda e, jmax=jmax, PT=PT: e.tensor_copy(a16s(PT, jmax), psT[:, 0:jmax]),
                         waits=[(smm, mm_pt)], inc=svv)
                    p3["pt"] = S.n(svv)
                    v_pt = S.n(svv)
                    for jt in range(it + 1):
                        S.op("tensor", lambda e, i=i, jt=jt, PT=PT:
                             e.matmul(ps2[:, 512:640], a16s(P_V + i * 2048 + jt * 128, 128),
                                      a16s(PT + jt * 128, 128),
                                      start=(jt == 0), stop=(jt == it), skip_group_check=True),
                             waits=([(svv, max(v_pt, p3["oev"]))] if jt == 0 else []),
                             inc=smm if jt == it else None)
                    mm_pv = S.n(smm)
                    S.op("vector", lambda e, i=i: e.tensor_copy(a16s(P_O + i * 128, 128), ps2[:, 512:640]),
                         waits=[(smm, mm_pv)], inc=svv)
                    p3["oev"] = S.n(svv)
                o_ev = S.n(svv)
                for i in range(GH):
                    for c in range(4):
                        S.op("tensor", lambda e, i=i, c=c:
                             e.matmul(ps1[:, c * 512:(c + 1) * 512], a16s(P_O + i * 128, 128),
                                      a16s(P_wo + i * 2048 + c * 512, 512),
                                      start=(i == 0), stop=(i == GH - 1), skip_group_check=True),
                             waits=([(svv, o_ev), (dG, wo_d), (sss, p3["exp"])] if (i == 0 and c == 0) else []),
                             inc=smm if (i == GH - 1 and c == 3) else None)
                mm_r = S.n(smm)
                for hh in range(2):
                    S.op("vector", lambda e, hh=hh, x1b=x1b:
                         e.tensor_tensor(x1b[:, hh * 1024:(hh + 1) * 1024],
                                         ps1[:, hh * 1024:(hh + 1) * 1024],
                                         x1b[:, hh * 1024:(hh + 1) * 1024], ALU.add),
                         waits=[(smm, mm_r), (dX, x1d)], inc=svv)
                p3["rev"] = S.n(svv)
                p3["x1f"] = S.n(svv)
                v_r = S.n(svv)
                S.op("scalar", lambda e, it=it, dst=dst, x1b=x1b:
                     e.dma_start(out=dst[it * 128:(it + 1) * 128, :], in_=x1b),
                     waits=[(svv, v_r)], inc=dX, amt=16)
                p3["x1st"] = S.n(dX)
            p3["wo_free"] = S.n(svv)
            p3["wqkv_free"] = S.n(smm)

        attn_done = S.n(dX)
        S.op("scalar", lambda e: e.nop(), waits=[(dX, attn_done)])

        ffn_pass(x1s, out_d, "f2")

        S.op("sync", lambda e: e.nop(), waits=[(dO, S.n(dO))])

        @block.sync
        def _(sync):
            S.emit("sync", sync)

        @block.vector
        def _(vector):
            S.emit("vector", vector)

        @block.scalar
        def _(scalar):
            S.emit("scalar", scalar)

        @block.tensor
        def _(tensor):
            S.emit("tensor", tensor)

        @block.gpsimd
        def _(gpsimd):
            S.emit("gpsimd", gpsimd)

    return nc


def _host_consts():
    ident = np.eye(128, dtype=bf16)
    perm = np.zeros((128, 128), np.float32)
    perm[(np.arange(128) + 64) % 128, np.arange(128)] = 1.0
    i = np.arange(128)[:, None]
    j = np.arange(128)[None, :]
    mask = np.where(j > i, np.float32(-1e30) * np.float32(128.0), np.float32(0.0)).astype(np.float32)
    half = 64
    freqs = np.power(np.float32(10000.0), -(np.arange(half, dtype=np.float32) / np.float32(half))).astype(np.float32)
    pos = np.arange(T, dtype=np.float32)[:, None]
    rad = (pos * freqs[None, :]).astype(np.float32)
    cos = np.cos(rad).astype(np.float32)
    sin = np.sin(rad).astype(np.float32)
    cc = np.ascontiguousarray(np.concatenate([cos.T, cos.T], axis=0).astype(np.float32))
    ss = np.ascontiguousarray(np.concatenate([sin.T, sin.T], axis=0).astype(np.float32))
    return ident, perm, mask, cc, ss


_CACHE = {}


def kernel(L0_attn_x, L0_attn_wq, L0_attn_wk, L0_attn_wv, L0_attn_wo,
           L0_ffn_w1, L0_ffn_w2, L0_ffn_w3):
    if "nc" not in _CACHE:
        _CACHE["nc"] = build_nc()
    nc = _CACHE["nc"]
    ident, perm, mask, cc, ss = _host_consts()
    w1t = np.ascontiguousarray(np.asarray(L0_ffn_w1, np.float32).T)
    w2t = np.ascontiguousarray(np.asarray(L0_ffn_w2, np.float32).T)
    w3t = np.ascontiguousarray(np.asarray(L0_ffn_w3, np.float32).T)
    wq = np.ascontiguousarray(np.asarray(L0_attn_wq, np.float32))
    wk = np.ascontiguousarray(np.asarray(L0_attn_wk, np.float32))
    wv = np.ascontiguousarray(np.asarray(L0_attn_wv, np.float32))
    wo = np.ascontiguousarray(np.asarray(L0_attn_wo, np.float32))
    x = np.asarray(L0_attn_x, np.float32)
    in_maps = []
    for c in range(8):
        in_maps.append({
            "x": np.ascontiguousarray(x[c]),
            "wq": wq, "wk": wk, "wv": wv, "wo": wo,
            "w1t": w1t, "w2t": w2t, "w3t": w3t,
            "ident": ident, "perm": perm, "mask": mask, "cch": cc, "ssh": ss,
        })
    res = run_bass_kernel_spmd(nc, in_maps, core_ids=list(range(8)))
    return np.stack([r["out"] for r in res.results]).astype(np.float32)


# version 25 (final)
# speedup vs baseline: 1.0310x; 1.0131x over previous
import contextlib
import numpy as np
import ml_dtypes
import concourse.bass as bass
import concourse.mybir as mybir
from concourse.bass_utils import run_bass_kernel_spmd

F32 = mybir.dt.float32
BF16 = mybir.dt.bfloat16
ALU = mybir.AluOpType
ACT = mybir.ActivationFunctionType
AX = mybir.AxisListType
bf16 = ml_dtypes.bfloat16

T = 2048
M = 2048
FF = 8192
H = 16
D = 128
NT = 16
NQ = 4
QT = 4
NFCC = 16
NFC = 64
NG = 4
GH = 4

MAGIC_K = 1.5 * 2.0 ** 82
MAGIC_C0 = 1.5 * 2.0 ** 23


class Sched:

    def __init__(self):
        self.ops = []
        self.counts = {}
        self.var = {}

    def n(self, sem):
        return self.counts.get(id(sem), 0)

    def op(self, eng, fn, waits=(), inc=None, amt=1, drain=False):
        incs = []
        if inc is not None:
            self.counts[id(inc)] = self.n(inc) + amt
            incs = [(inc, amt)]
        ws = tuple((s, v) for (s, v) in waits if v > 0)
        self.ops.append((eng, fn, ws, incs, drain))

    def emit(self, name, eng):
        seen = {}
        for (en, fn, waits, incs, drain) in self.ops:
            if en != name:
                continue
            for (sem, v) in waits:
                if seen.get(id(sem), 0) < v:
                    eng.wait_ge(sem, v)
                    seen[id(sem)] = v
            if drain:
                eng.drain()
            instr = fn(eng)
            for (sem, k) in incs:
                instr.then_inc(sem, k)


def build_nc():
    nc = bass.Bass("TRN2", target_bir_lowering=False, dynamic_dma_scratch_size=16384)

    def din(name, shape, dt):
        return nc.dram_tensor(name, shape, dt, kind="ExternalInput").ap()

    def dint(name, shape, dt):
        return nc.dram_tensor(name, shape, dt, kind="Internal").ap()

    x_in = din("x", [T, M], F32)
    wq_in = din("wq", [M, H, D], F32)
    wk_in = din("wk", [M, H, D], F32)
    wv_in = din("wv", [M, H, D], F32)
    wo_in = din("wo", [H, D, M], F32)
    w1t_in = din("w1t", [M, FF], F32)
    w2t_in = din("w2t", [M, FF], F32)
    w3t_in = din("w3t", [FF, M], F32)
    ident_in = din("ident", [128, 128], BF16)
    perm_in = din("perm", [128, 128], F32)
    mask_in = din("mask", [128, 128], F32)
    cch_in = din("cch", [128, T], F32)
    ssh_in = din("ssh", [128, T], F32)

    t1s = dint("t1s", [M, FF], BF16)
    t2s = dint("t2s", [M, FF], BF16)
    t3s = dint("t3s", [FF, M], BF16)
    x1s = dint("x1s", [T, M], F32)
    x1s2 = dint("x1s2", [T, M], F32)
    ln2s = dint("ln2s", [M, T], BF16)

    out_d = nc.dram_tensor("out", [T, M], F32, kind="ExternalOutput").ap()

    t1v = t1s.rearrange("(a p) f -> p a f", p=128)
    t2v = t2s.rearrange("(a p) f -> p a f", p=128)
    ln2v = ln2s.rearrange("(a p) t -> p a t", p=128)

    es = contextlib.ExitStack()
    with es:
        def sbt(name, cols, dt):
            return es.enter_context(nc.sbuf_tensor(name, [128, cols], dt))

        A16 = 65536
        A32 = 10240
        a16 = sbt("a16", A16, BF16)
        a32 = sbt("a32", A32, F32)
        cc_t = sbt("cc_t", T, F32)
        ss_t = sbt("ss_t", T, F32)
        ident = sbt("identsb", 128, BF16)
        perm = sbt("permsb", 128, F32)
        mask = sbt("masksb", 128, F32)
        stat = sbt("stat", 128, F32)
        part = sbt("part", 256, F32)
        ones = sbt("ones", 128, F32)

        ps1 = es.enter_context(nc.psum_tensor("ps1", [128, 2048], F32))
        ps2 = es.enter_context(nc.psum_tensor("ps2", [128, 1024], F32))
        psT = es.enter_context(nc.psum_tensor("psT", [128, 2048], BF16))

        dW = es.enter_context(nc.semaphore("dW"))
        dX = es.enter_context(nc.semaphore("dX"))
        dG = es.enter_context(nc.semaphore("dG"))
        dO = es.enter_context(nc.semaphore("dO"))
        smm = es.enter_context(nc.semaphore("smm"))
        svv = es.enter_context(nc.semaphore("svv"))
        sss = es.enter_context(nc.semaphore("sss"))
        sgg = es.enter_context(nc.semaphore("sgg"))

        block = es.enter_context(nc.Block())
        S = Sched()

        _si = [0]

        def stcol():
            i = _si[0]
            _si[0] += 1
            assert _si[0] <= 128
            return stat[:, i:i + 1]

        C_ap = stcol()
        flag = stcol()
        c127 = stcol()
        eps5 = stcol()
        eps6 = stcol()
        one_c = stcol()
        rs_w = [stcol() for _ in range(3)]
        sw_w = [stcol() for _ in range(3)]
        g1c = [stcol() for _ in range(QT)]
        g2c = [stcol() for _ in range(QT)]
        c1c = [stcol() for _ in range(QT)]
        c2c = [stcol() for _ in range(QT)]
        c1h = [stcol() for _ in range(QT)]
        coef3 = [stcol() for _ in range(QT)]
        invc = stcol()
        c2a = stcol()
        rden = stcol()
        rden2 = stcol()
        pidx = stcol()
        jv = stcol()
        sgn_c = stcol()
        fr_c = stcol()
        tA = [stcol() for _ in range(8)]

        ssqp = [part[:, i * NFCC:(i + 1) * NFCC] for i in range(QT)]
        hmxp = [part[:, 64 + i * NFCC:64 + (i + 1) * NFCC] for i in range(QT)]
        wpart = part[:, 128:192]
        den_c = part[:, 192:193]
        den_p = [part[:, 192:193], part[:, 193:194]]
        rdenp = [rden, rden2]

        A_h = 0
        A_nT = 32768
        A_w1 = 40960
        A_w2 = 57344
        A_nscr = 57344
        A_tern = 61440
        P_K = 0
        P_V = 8192
        P_Q = 16384
        P_wo = 24576
        P_wqkv = 32768
        P_ln2c = 45056
        P_Pa = [32768, 38912]
        P_Pb = [34816, 40960]
        P_PT = [36864, 43008]
        P_O = 61952
        P_vt = 62464
        P_lt = 61440

        B_x = 0
        B_vq = 2048
        B_dump = 4096
        B_sil = 5120
        B_rx = 6144
        B_kq = 2048
        B_x1 = 0

        def a16s(off, n):
            return a16[:, off:off + n]

        def a32s(off, n):
            return a32[:, off:off + n]

        S.op("sync", lambda e: e.dma_start(out=ident[:], in_=ident_in[:]), inc=dW, amt=16)
        S.op("sync", lambda e: e.dma_start(out=perm[:], in_=perm_in[:]), inc=dW, amt=16)
        S.op("sync", lambda e: e.dma_start(out=mask[:], in_=mask_in[:]), inc=dW, amt=16)
        S.op("sync", lambda e: e.dma_start(out=cc_t[:], in_=cch_in[:]), inc=dW, amt=16)
        S.op("sync", lambda e: e.dma_start(out=ss_t[:], in_=ssh_in[:]), inc=dW, amt=16)
        w0 = S.n(dW)

        S.op("vector", lambda e: e.memset(tA[0], 2.0 ** -30))
        S.op("vector", lambda e: e.tensor_scalar(tA[1], tA[0], 1.0, None, ALU.add), drain=True)
        S.op("vector", lambda e: e.tensor_scalar(tA[2], tA[1], -1.0, None, ALU.add), drain=True)
        S.op("vector", lambda e: e.tensor_scalar(C_ap, tA[2], MAGIC_K, MAGIC_C0, ALU.mult, ALU.add), drain=True)
        S.op("vector", lambda e: e.tensor_scalar(flag, tA[2], 2.0 ** 30, None, ALU.mult), drain=True)
        S.op("vector", lambda e: e.memset(c127, 127.0))
        S.op("vector", lambda e: e.memset(eps5, 1e-5))
        S.op("vector", lambda e: e.memset(eps6, 1e-6))
        S.op("vector", lambda e: e.memset(one_c, 1.0))
        S.op("vector", lambda e: e.memset(ones[:], 1.0), inc=svv)

        rope_rad = a32s(0, 2048)
        rope_t1 = a32s(2048, 2048)
        rope_t2 = a32s(4096, 2048)

        S.op("gpsimd", lambda e: e.iota(pidx, pattern=[[0, 1]], base=0, channel_multiplier=1,
                                        allow_small_or_imprecise_dtypes=True), inc=sgg)
        S.op("gpsimd", lambda e: e.iota(rope_rad[:], pattern=[[1, 2048]], base=0, channel_multiplier=0,
                                        allow_small_or_imprecise_dtypes=True), inc=sgg)
        g0 = S.n(sgg)
        S.op("vector", lambda e: e.tensor_scalar(jv, pidx, 64.0, None, ALU.is_ge),
             waits=[(sgg, g0)], drain=True)
        S.op("vector", lambda e: e.tensor_scalar(sgn_c, jv, 2.0, -1.0, ALU.mult, ALU.add), drain=True)
        S.op("vector", lambda e: e.scalar_tensor_tensor(jv, jv, -64.0, pidx, ALU.mult, ALU.add),
             inc=svv, drain=True)
        v_j = S.n(svv)
        S.op("scalar", lambda e: e.activation(tA[0], one_c, ACT.Ln, scale=10000.0), waits=[(svv, v_j)], inc=sss)
        s_ln = S.n(sss)
        S.op("vector", lambda e: e.tensor_scalar(tA[0], tA[0], -1.0 / 64.0, None, ALU.mult),
             waits=[(sss, s_ln)], inc=svv, drain=True)
        S.op("scalar", lambda e: e.activation(fr_c, jv, ACT.Exp, scale=tA[0]),
             waits=[(svv, S.n(svv))], inc=sss)
        s_fr = S.n(sss)
        S.op("scalar", lambda e: e.activation(tA[4], one_c, ACT.Arctan), inc=sss)
        s_at = S.n(sss)
        S.op("vector", lambda e: e.tensor_scalar(tA[4], tA[4], 4.0, None, ALU.mult), waits=[(sss, s_at)], drain=True)
        S.op("vector", lambda e: e.tensor_scalar(tA[3], tA[4], -1.0, None, ALU.mult), drain=True)
        S.op("vector", lambda e: e.tensor_scalar(tA[6], tA[4], 0.5, None, ALU.mult), drain=True)
        S.op("vector", lambda e: e.tensor_scalar(tA[7], tA[4], -2.0, None, ALU.mult), drain=True)
        S.op("vector", lambda e: e.tensor_scalar(tA[5], tA[4], 2.0, None, ALU.mult), drain=True)
        S.op("vector", lambda e: e.reciprocal(tA[5], tA[5]), drain=True)
        S.op("vector", lambda e: e.tensor_scalar(rope_rad[:], rope_rad[:], fr_c, None, ALU.mult),
             waits=[(sss, s_fr)], inc=svv, drain=True)

        def rope_dev(tbl, phase_half_pi):
            if phase_half_pi:
                S.op("vector", lambda e: e.tensor_scalar(rope_t1[:], rope_rad[:], tA[6], None, ALU.add), drain=True)
            else:
                S.op("vector", lambda e: e.tensor_copy(rope_t1[:], rope_rad[:]), drain=True)
            S.op("vector", lambda e: e.tensor_scalar(rope_t2[:], rope_t1[:], tA[5], None, ALU.mult), drain=True)
            S.op("vector", lambda e: e.tensor_scalar(rope_t2[:], rope_t2[:], C_ap, C_ap, ALU.add, ALU.subtract), drain=True)
            S.op("vector", lambda e: e.tensor_scalar(rope_t2[:], rope_t2[:], tA[7], None, ALU.mult), drain=True)
            S.op("vector", lambda e: e.tensor_tensor(rope_t2[:], rope_t2[:], rope_t1[:], ALU.add), drain=True)
            S.op("vector", lambda e: e.tensor_scalar(rope_t2[:], rope_t2[:], tA[4], None, ALU.min), drain=True)
            S.op("vector", lambda e: e.tensor_scalar(rope_t2[:], rope_t2[:], tA[3], None, ALU.max),
                 inc=svv, drain=True)
            vv0 = S.n(svv)
            S.op("scalar", lambda e: e.activation(rope_t1[:], rope_t2[:], ACT.Sin), waits=[(svv, vv0)], inc=sss)
            ss0 = S.n(sss)
            S.op("vector", lambda e, t=tbl: e.tensor_tensor(rope_t2[:], rope_t1[:], t, ALU.subtract),
                 waits=[(sss, ss0), (dW, w0)], drain=True)
            S.op("vector", lambda e, t=tbl: e.scalar_tensor_tensor(rope_t1[:], rope_t2[:], flag, t, ALU.mult, ALU.add),
                 drain=True)
            S.op("vector", lambda e, t=tbl: e.tensor_copy(t, rope_t1[:]), inc=svv, drain=True)

        rope_dev(ss_t[:], False)
        rope_dev(cc_t[:], True)
        S.op("vector", lambda e: e.tensor_scalar(ss_t[:], ss_t[:], sgn_c, None, ALU.mult), inc=svv, drain=True)
        v_rope = S.n(svv)

        wsrc = [w1t_in, w2t_in, w3t_in]
        tdst = [t1s, t2s, t3s]
        NCH = 32

        def chunk_ap(t, idx, wi):
            cols = FF if wi < 2 else M
            ccw = min(4096, cols)
            per = cols // ccw
            r, c = idx // per, idx % per
            return t[r * 128:(r + 1) * 128, c * ccw:(c + 1) * ccw]

        sc_state = {}

        def plan_scale_chunk(wi, idx, lastuse, loads):
            par = idx % 2
            buf = a32s(par * 4096, 4096)
            ccw = 4096 if wi < 2 else 2048
            buf = a32s(par * 4096, ccw)
            S.op("sync", lambda e, wi=wi, idx=idx, buf=buf:
                 e.dma_start(out=buf, in_=chunk_ap(wsrc[wi], idx, wi)),
                 waits=[(svv, lastuse[par])], inc=dW, amt=16)
            wd = S.n(dW)
            col = idx % 64
            S.op("vector", lambda e, buf=buf, col=col: e.reduce_sum(
                wpart[:, col:col + 1], buf, AX.X, apply_absolute_value=True),
                waits=[(dW, wd)], inc=svv)
            lastuse[par] = S.n(svv)

        def plan_scale_final(wi):
            n = 32 if wi < 2 else 64
            S.op("vector", lambda e, n=n: e.reduce_sum(tA[1], wpart[:, 0:n], AX.X), inc=svv, drain=True)
            vsum = S.n(svv)
            S.op("tensor", lambda e: e.matmul(ps2[:, 0:1], ones[:], tA[1], start=True, stop=True),
                 waits=[(svv, vsum)], inc=smm)
            mm0 = S.n(smm)
            S.op("vector", lambda e, wi=wi: e.tensor_scalar(rs_w[wi], ps2[:, 0:1], 1.0 / 16777216.0, None, ALU.mult),
                 waits=[(smm, mm0)], drain=True)
            S.op("vector", lambda e, wi=wi: e.tensor_scalar(rs_w[wi], rs_w[wi], 1e-5, None, ALU.max), drain=True)
            S.op("vector", lambda e, wi=wi: e.reciprocal(sw_w[wi], rs_w[wi]), inc=svv, drain=True)

        def plan_tern_chunk(wi, idx, lastuse, laststore):
            par = idx % 2
            fbuf = a32s(8192 + par * 1024, 1024)
            tbuf = a16s(A_tern + par * 1024, 1024)
            rows, cols = (M, FF) if wi < 2 else (FF, M)
            per = cols // 1024
            r, c = idx // per, idx % per
            S.op("sync", lambda e, wi=wi, r=r, c=c, fbuf=fbuf:
                 e.dma_start(out=fbuf, in_=wsrc[wi][r * 128:(r + 1) * 128, c * 1024:(c + 1) * 1024]),
                 waits=[(svv, lastuse[par]), (dG, laststore[par])], inc=dW, amt=16)
            wd = S.n(dW)
            S.op("vector", lambda e, fbuf=fbuf, wi=wi:
                 e.tensor_scalar(fbuf, fbuf, sw_w[wi], C_ap, ALU.mult, ALU.add),
                 waits=[(dW, wd)], drain=True)
            S.op("vector", lambda e, fbuf=fbuf: e.tensor_scalar(fbuf, fbuf, C_ap, None, ALU.subtract),
                 drain=True)
            S.op("vector", lambda e, fbuf=fbuf, tbuf=tbuf:
                 e.tensor_scalar(tbuf, fbuf, 1.0, -1.0, ALU.min, ALU.max), inc=svv, drain=True)
            vv = S.n(svv)
            lastuse[par] = vv
            S.op("gpsimd", lambda e, wi=wi, r=r, c=c, tbuf=tbuf:
                 e.dma_start(out=tdst[wi][r * 128:(r + 1) * 128, c * 1024:(c + 1) * 1024], in_=tbuf),
                 waits=[(svv, vv)], inc=dG, amt=16)
            laststore[par] = S.n(dG)

        sc_use = [max(S.n(svv), 1), max(S.n(svv), 1)]
        for idx in range(32):
            plan_scale_chunk(0, idx, sc_use, None)
        plan_scale_final(0)
        tn_use = [S.n(svv), S.n(svv)]
        tn_st = [0, 0]
        for wi in range(3):
            nsc = 32 if wi == 0 else 64
            step = 128 // nsc
            for idx in range(128):
                plan_tern_chunk(wi, idx, tn_use, tn_st)
                if wi < 2 and idx % step == 0 and idx // step < nsc:
                    plan_scale_chunk(wi + 1, idx // step, sc_use, None)
            if wi < 2:
                plan_scale_final(wi + 1)
            if wi == 0:
                g_t1 = S.n(dG)
            elif wi == 1:
                g_t2 = S.n(dG)
        g_tern = S.n(dG)
        g_t3 = g_tern

        def ffn_pass(xsrc, ydst, tag):
            V = {"xfree": S.n(svv),
                 "rx0": S.n(svv), "rx1": S.n(svv), "b1ev": S.n(svv),
                 "rxst0": S.n(dO), "rxst1": S.n(dO), "ring": 0,
                 "t1free": S.n(smm), "t2free": S.n(smm),
                 "bfree_v": [S.n(svv)] * 6, "bfree_s": [S.n(sss)] * 6,
                 "silfree": [S.n(svv)] * 4}
            t3last = [S.n(smm)] * 3
            for q in range(NQ):
                aprep_v = 0
                aprep_s = 0
                for t in range(QT):
                    tti = q * QT + t
                    xbuf = a32s(B_x, 2048)
                    S.op("scalar", lambda e, xsrc=xsrc, tti=tti, xbuf=xbuf:
                         e.dma_start(out=xbuf, in_=xsrc[tti * 128:(tti + 1) * 128, :]),
                         waits=[(svv, V["xfree"])], inc=dX, amt=16)
                    xd = S.n(dX)
                    S.op("vector", lambda e, xbuf=xbuf: e.reduce_sum(tA[0], xbuf, AX.X),
                         waits=[(dX, xd)], drain=True)
                    S.op("scalar", lambda e, xbuf=xbuf: e.activation(a32s(B_dump, 1024), xbuf[:, 0:1024],
                         ACT.Square, accum_out=tA[5]), waits=[(dX, xd)])
                    S.op("scalar", lambda e, xbuf=xbuf: e.activation(a32s(B_dump, 1024), xbuf[:, 1024:2048],
                         ACT.Square, accum_out=tA[6]), inc=sss)
                    sq0 = S.n(sss)
                    S.op("vector", lambda e, xbuf=xbuf: e.reduce_max(tA[2], xbuf, AX.X), drain=True)
                    S.op("vector", lambda e, xbuf=xbuf: e.tensor_reduce(tA[3], xbuf, AX.X, ALU.min), drain=True)
                    S.op("vector", lambda e: e.tensor_scalar(tA[0], tA[0], 1.0 / M, None, ALU.mult),
                         waits=[(sss, sq0)], drain=True)
                    S.op("vector", lambda e: e.tensor_tensor(tA[5], tA[5], tA[6], ALU.add), drain=True)
                    S.op("vector", lambda e: e.tensor_scalar(tA[5], tA[5], 1.0 / M, None, ALU.mult), drain=True)
                    S.op("vector", lambda e: e.tensor_scalar(tA[1], tA[0], tA[0], None, ALU.mult), drain=True)
                    S.op("vector", lambda e: e.scalar_tensor_tensor(tA[1], tA[1], -1.0, tA[5], ALU.mult, ALU.add), drain=True)
                    S.op("vector", lambda e: e.tensor_tensor(tA[2], tA[2], tA[0], ALU.subtract), drain=True)
                    S.op("vector", lambda e: e.tensor_tensor(tA[3], tA[0], tA[3], ALU.subtract), drain=True)
                    S.op("vector", lambda e: e.tensor_tensor(tA[2], tA[2], tA[3], ALU.max), drain=True)
                    S.op("vector", lambda e, t=t: e.reciprocal(c1c[t], tA[2]), drain=True)
                    S.op("vector", lambda e, t=t: e.tensor_scalar(c1c[t], c1c[t], 127.0, None, ALU.mult), drain=True)
                    S.op("vector", lambda e, t=t: e.tensor_scalar(c2c[t], tA[0], c1c[t], -1.0, ALU.mult, ALU.mult), drain=True)
                    S.op("vector", lambda e: e.tensor_tensor(tA[4], tA[1], eps5, ALU.add), inc=svv, drain=True)
                    vst = S.n(svv)
                    S.op("scalar", lambda e: e.activation(tA[4], tA[4], ACT.Ln), waits=[(svv, vst)])
                    S.op("scalar", lambda e: e.activation(tA[4], tA[4], ACT.Exp, scale=-0.5), inc=sss)
                    sst = S.n(sss)
                    S.op("vector", lambda e: e.tensor_scalar(tA[6], tA[4], tA[4], None, ALU.mult),
                         waits=[(sss, sst)], drain=True)
                    S.op("vector", lambda e: e.tensor_tensor(tA[6], tA[1], tA[6], ALU.mult), drain=True)
                    S.op("vector", lambda e: e.tensor_tensor(tA[6], tA[6], eps6, ALU.add), inc=svv, drain=True)
                    vst = S.n(svv)
                    S.op("scalar", lambda e: e.activation(tA[6], tA[6], ACT.Ln), waits=[(svv, vst)])
                    S.op("scalar", lambda e: e.activation(tA[6], tA[6], ACT.Exp, scale=-0.5), inc=sss)
                    sst = S.n(sss)
                    S.op("vector", lambda e: e.tensor_scalar(tA[7], tA[2], tA[4], None, ALU.mult),
                         waits=[(sss, sst)], drain=True)
                    S.op("vector", lambda e: e.tensor_scalar(tA[7], tA[7], tA[6], None, ALU.mult), drain=True)
                    S.op("vector", lambda e: e.tensor_scalar(tA[7], tA[7], 1e-5, 1.0 / 127.0, ALU.max, ALU.mult), drain=True)
                    S.op("vector", lambda e, t=t: e.tensor_scalar(g1c[t], tA[7], rs_w[0], None, ALU.mult), drain=True)
                    S.op("vector", lambda e, t=t: e.tensor_scalar(g2c[t], tA[7], rs_w[1], None, ALU.mult), inc=svv, drain=True)
                    for hh in range(2):
                        vq = a32s(B_vq + hh * 1024, 1024)
                        nq = a16s(A_nscr + (t % 2) * 2048 + hh * 1024, 1024)
                        S.op("vector", lambda e, xbuf=xbuf, vq=vq, t=t, hh=hh:
                             e.tensor_scalar(vq, xbuf[:, hh * 1024:(hh + 1) * 1024], c1c[t], c2c[t], ALU.mult, ALU.add),
                             drain=True)
                        S.op("vector", lambda e, vq=vq, nq=nq:
                             e.tensor_scalar(nq, vq, C_ap, C_ap, ALU.add, ALU.subtract), inc=svv, drain=True)
                    V["xfree"] = S.n(svv)
                    vq_done = S.n(svv)
                    nq_full = a16s(A_nscr + (t % 2) * 2048, 2048)
                    for mc in range(16):
                        S.op("tensor", lambda e, nq=nq_full, mc=mc:
                             e.transpose(psT[:, mc * 128:(mc + 1) * 128], nq[:, mc * 128:(mc + 1) * 128], ident[:]),
                             waits=[(svv, vq_done)] if mc == 0 else [],
                             inc=smm if mc == 15 else None)
                    tmm = S.n(smm)
                    nT_view = a16s(A_nT, 8192).rearrange("p (a t) -> p a t", t=512)
                    S.op("vector", lambda e, t=t, nv=nT_view: e.tensor_copy(
                        nv[:, :, t * 128:(t + 1) * 128], psT[:, 0:2048]),
                        waits=[(smm, tmm)], inc=svv)
                aprep_v = S.n(svv)
                aprep_s = S.n(sss)

                for fcc in range(NFCC):
                    t1b = a16s(A_w1, 8192)
                    t2b = a16s(A_w1 + 8192, 8192)
                    S.op("sync", lambda e, fcc=fcc:
                         e.dma_start(out=a16s(A_w1, 8192).rearrange("p (a f) -> p a f", f=512),
                                     in_=t1v[:, :, fcc * 512:(fcc + 1) * 512]),
                         waits=[(smm, V["t1free"]), (dG, g_t1)], inc=dW, amt=16)
                    w1d = S.n(dW)
                    S.op("sync", lambda e, fcc=fcc:
                         e.dma_start(out=a16s(A_w1 + 8192, 8192).rearrange("p (a f) -> p a f", f=512),
                                     in_=t2v[:, :, fcc * 512:(fcc + 1) * 512]),
                         waits=[(smm, V["t2free"]), (dG, g_t2)], inc=dW, amt=16)
                    w2d = S.n(dW)
                    def bank_ap(b):
                        return ps1[:, b * 512:(b + 1) * 512] if b < 4 else ps2[:, (b - 4) * 512:(b - 3) * 512]
                    u1bank = [(V["ring"] + i) % 6 for i in range(4)]
                    u2bank = [(V["ring"] + 4 + i) % 6 for i in range(4)]
                    V["ring"] = (V["ring"] + 8) % 6
                    mmu1 = [0] * 4
                    for t in range(QT):
                        b = u1bank[t]
                        pu1 = bank_ap(b)
                        for mc in range(16):
                            lhs = a16s(A_nT + mc * 512 + t * 128, 128)
                            S.op("tensor", lambda e, lhs=lhs, t1b=t1b, mc=mc, pu1=pu1:
                                 e.matmul(pu1, lhs, t1b[:, mc * 512:(mc + 1) * 512],
                                          start=(mc == 0), stop=(mc == 15), skip_group_check=True),
                                 waits=([(dW, w1d), (svv, max(aprep_v, V["bfree_v"][b])),
                                         (sss, V["bfree_s"][b])] if mc == 0 else []),
                                 inc=smm if mc == 15 else None)
                        mmu1[t] = S.n(smm)
                    V["t1free"] = S.n(smm)
                    for t in range(QT):
                        b = u2bank[t]
                        pu2 = bank_ap(b)
                        for mc in range(16):
                            lhs = a16s(A_nT + mc * 512 + t * 128, 128)
                            S.op("tensor", lambda e, lhs=lhs, t2b=t2b, mc=mc, pu2=pu2:
                                 e.matmul(pu2, lhs, t2b[:, mc * 512:(mc + 1) * 512],
                                          start=(mc == 0), stop=(mc == 15), skip_group_check=True),
                                 waits=([(dW, w2d), (svv, V["bfree_v"][b]), (sss, V["bfree_s"][b])]
                                        if mc == 0 else []),
                                 inc=smm if mc == 15 else None)
                        mm_u2 = S.n(smm)
                        silb = a32s([B_sil, B_sil + 512, 7168, 7680][t], 512)
                        pu1 = bank_ap(u1bank[t])
                        S.op("scalar", lambda e, pu1=pu1, silb=silb, t=t:
                             e.activation(silb, pu1, ACT.Silu, scale=g1c[t]),
                             waits=[(smm, mmu1[t]), (sss, aprep_s), (svv, V["silfree"][t])], inc=sss)
                        V["bfree_s"][u1bank[t]] = S.n(sss)
                        s_sil = S.n(sss)
                        hslice = a16s(A_h + t * 8192 + fcc * 512, 512)
                        S.op("vector", lambda e, silb=silb, pu2=pu2, hs=hslice:
                             e.tensor_tensor(hs, silb, pu2, ALU.mult),
                             waits=[(sss, s_sil), (smm, mm_u2)], inc=svv)
                        V["bfree_v"][u2bank[t]] = S.n(svv)
                        V["silfree"][t] = S.n(svv)
                    V["t2free"] = S.n(smm)
                all_ev = S.n(svv)
                for t in range(QT):
                    for c in range(4):
                        S.op("vector", lambda e, t=t, c=c: e.reduce_max(
                            hmxp[t][:, c:c + 1], a16s(A_h + t * 8192 + c * 2048, 2048), AX.X,
                            apply_absolute_value=True),
                            waits=[(svv, all_ev)], inc=svv if c == 3 else None)
                    for c in range(8):
                        S.op("scalar", lambda e, t=t, c=c: e.activation(
                            a32s(B_dump, 1024), a16s(A_h + t * 8192 + c * 1024, 1024),
                            ACT.Square, accum_out=ssqp[t][:, c:c + 1]),
                            waits=[(svv, all_ev)] if c == 0 else [], inc=sss if c == 7 else None)
                hstat_v = S.n(svv)
                hstat_s = S.n(sss)

                for t in range(QT):
                    S.op("vector", lambda e, t=t: e.reduce_max(tA[0], hmxp[t][:, 0:4], AX.X),
                         waits=[(svv, hstat_v), (sss, hstat_s)], drain=True)
                    S.op("vector", lambda e, t=t: e.reduce_sum(tA[1], ssqp[t][:, 0:8], AX.X), drain=True)
                    S.op("vector", lambda e, t=t: e.reciprocal(c1h[t], tA[0]), drain=True)
                    S.op("vector", lambda e, t=t: e.tensor_scalar(c1h[t], c1h[t], 127.0, None, ALU.mult), drain=True)
                    S.op("vector", lambda e, t=t: e.tensor_scalar(tA[2], tA[1], g2c[t], g2c[t], ALU.mult, ALU.mult), drain=True)
                    S.op("vector", lambda e: e.tensor_scalar(tA[2], tA[2], 1.0 / FF, None, ALU.mult), drain=True)
                    S.op("vector", lambda e: e.tensor_tensor(tA[2], tA[2], eps6, ALU.add), inc=svv, drain=True)
                    vst = S.n(svv)
                    S.op("scalar", lambda e: e.activation(tA[2], tA[2], ACT.Ln), waits=[(svv, vst)])
                    S.op("scalar", lambda e: e.activation(tA[2], tA[2], ACT.Exp, scale=-0.5), inc=sss)
                    sst = S.n(sss)
                    S.op("vector", lambda e, t=t: e.tensor_scalar(tA[3], tA[0], g2c[t], None, ALU.mult),
                         waits=[(sss, sst)], drain=True)
                    S.op("vector", lambda e: e.tensor_scalar(tA[3], tA[3], tA[2], None, ALU.mult), drain=True)
                    S.op("vector", lambda e: e.tensor_scalar(tA[3], tA[3], 1e-5, 0.5 / 127.0, ALU.max, ALU.mult), drain=True)
                    S.op("vector", lambda e, t=t: e.tensor_scalar(coef3[t], tA[3], rs_w[2], None, ALU.mult),
                         inc=svv, drain=True)
                v_hstats = S.n(svv)

                prev_tq = 0
                for t in range(QT):
                    for fg in range(4):
                        hs = a16s(A_h + t * 8192 + fg * 2048, 2048)
                        vq = a32s(B_vq, 2048)
                        nq = a16s(A_nscr + (fg % 2) * 2048, 2048)
                        S.op("vector", lambda e, hs=hs, vq=vq, t=t:
                             e.tensor_scalar(vq, hs, c1h[t], C_ap, ALU.mult, ALU.add),
                             waits=[(svv, v_hstats)], drain=True)
                        S.op("vector", lambda e, vq=vq, nq=nq:
                             e.tensor_scalar(nq, vq, C_ap, None, ALU.subtract), inc=svv, drain=True)
                        v_nq = S.n(svv)
                        for b in range(16):
                            S.op("tensor", lambda e, nq=nq, b=b:
                                 e.transpose(psT[:, b * 128:(b + 1) * 128], nq[:, b * 128:(b + 1) * 128], ident[:]),
                                 waits=[(svv, v_nq)] if b == 0 else [],
                                 inc=smm if b == 15 else None)
                        prev_tq = S.n(smm)
                        S.op("vector", lambda e, hs=hs: e.tensor_copy(hs, psT[:, 0:2048]),
                             waits=[(smm, prev_tq)], inc=svv)
                v_b0 = S.n(svv)

                for mq in range(4):
                    mmfc = {}
                    for fg in range(16):
                        par = fg % 3
                        t3b = a16s(A_w1 + par * 2048, 2048)
                        t3g = t3s.rearrange("(a p) m -> p a m", p=128)
                        S.op("sync", lambda e, fg=fg, mq=mq, par=par, t3g=t3g:
                             e.dma_start(out=a16s(A_w1 + par * 2048, 2048).rearrange("p (a m) -> p a m", m=512),
                                         in_=t3g[:, fg * 4:(fg + 1) * 4, mq * 512:(mq + 1) * 512]),
                             waits=[(smm, mmfc.get(fg - 3, t3last[par])), (svv, v_b0), (dG, g_t3)],
                             inc=dW, amt=16)
                        t3d = S.n(dW)
                        for b in range(4):
                            fc = fg * 4 + b
                            for t in range(QT):
                                lhs = a16s(A_h + t * 8192 + fc * 128, 128)
                                S.op("tensor", lambda e, lhs=lhs, t3b=t3b, b=b, t=t, fc=fc:
                                     e.matmul(ps1[:, t * 512:(t + 1) * 512], lhs, t3b[:, b * 512:(b + 1) * 512],
                                              start=(fc == 0), stop=(fc == NFC - 1), skip_group_check=True),
                                     waits=([(dW, t3d), (svv, V["b1ev"])] if (t == 0 and b == 0 and fg == 0) else
                                            ([(dW, t3d)] if (t == 0 and b == 0) else [])),
                                     inc=smm if (t == QT - 1 and b == 3) else None)
                        mmfc[fg] = S.n(smm)
                    mm_y = S.n(smm)
                    t3last = [mm_y] * 3
                    for t in range(QT):
                        tti = q * QT + t
                        rx = a32s(B_rx + (t % 2) * 512, 512)
                        S.op("scalar", lambda e, xsrc=xsrc, tti=tti, mq=mq, rx=rx:
                             e.dma_start(out=rx, in_=xsrc[tti * 128:(tti + 1) * 128, mq * 512:(mq + 1) * 512]),
                             waits=[(svv, V["rx%d" % (t % 2)]), (dO, V["rxst%d" % (t % 2)])], inc=dX, amt=16)
                        rxd = S.n(dX)
                        S.op("vector", lambda e, rx=rx, t=t:
                             e.scalar_tensor_tensor(rx, ps1[:, t * 512:(t + 1) * 512], coef3[t], rx, ALU.mult, ALU.add),
                             waits=[(smm, mm_y), (dX, rxd)], inc=svv)
                        V["rx%d" % (t % 2)] = S.n(svv)
                        v_rx = S.n(svv)
                        S.op("scalar", lambda e, ydst=ydst, tti=tti, mq=mq, rx=rx:
                             e.dma_start(out=ydst[tti * 128:(tti + 1) * 128, mq * 512:(mq + 1) * 512], in_=rx),
                             waits=[(svv, v_rx)], inc=dO, amt=16)
                        V["rxst%d" % (t % 2)] = S.n(dO)
                    V["b1ev"] = S.n(svv)
                V["t1free"] = S.n(smm)
                V["t2free"] = S.n(smm)
                V["bfree_v"] = [S.n(svv)] * 6
                V["bfree_s"] = [S.n(sss)] * 6

        ffn_pass(x_in, x1s, "f1")
        ffn1_done_dO = S.n(dO)

        p3 = {"wqkv_free": S.n(smm), "ln0": S.n(smm), "ln1": S.n(smm),
              "wo_free": S.n(svv), "proj_ev": S.n(svv), "x1f": S.n(svv),
              "rev": S.n(svv), "pn0": S.n(svv), "pn1": S.n(svv), "pt": S.n(svv), "oev": S.n(svv),
              "exp": S.n(sss), "xprep": S.n(svv)}

        prep_mm = S.n(smm)
        prep_free = {0: S.n(svv), 1: S.n(svv)}
        prep_ta = S.n(svv)
        for tti in range(NT):
            xbuf = a32s([B_x, B_vq][tti % 2], 2048)
            S.op("scalar", lambda e, tti=tti, xbuf=xbuf:
                 e.dma_start(out=xbuf, in_=x1s[tti * 128:(tti + 1) * 128, :]),
                 waits=[(svv, prep_free[tti % 2]), (dO, ffn1_done_dO)], inc=dX, amt=16)
            xd = S.n(dX)
            S.op("vector", lambda e, xbuf=xbuf: e.reduce_sum(tA[0], xbuf, AX.X),
                 waits=[(dX, xd)], drain=True)
            S.op("scalar", lambda e, xbuf=xbuf: e.activation(a32s(B_dump, 1024), xbuf[:, 0:1024],
                 ACT.Square, accum_out=tA[5]), waits=[(dX, xd), (svv, prep_ta)])
            S.op("scalar", lambda e, xbuf=xbuf: e.activation(a32s(B_dump, 1024), xbuf[:, 1024:2048],
                 ACT.Square, accum_out=tA[6]), inc=sss)
            sq0 = S.n(sss)
            S.op("vector", lambda e: e.tensor_scalar(tA[0], tA[0], 1.0 / M, None, ALU.mult),
                 waits=[(sss, sq0)], drain=True)
            S.op("vector", lambda e: e.tensor_tensor(tA[5], tA[5], tA[6], ALU.add), drain=True)
            S.op("vector", lambda e: e.tensor_scalar(tA[5], tA[5], 1.0 / M, None, ALU.mult), drain=True)
            S.op("vector", lambda e: e.tensor_scalar(tA[1], tA[0], tA[0], None, ALU.mult), drain=True)
            S.op("vector", lambda e: e.scalar_tensor_tensor(tA[1], tA[1], -1.0, tA[5], ALU.mult, ALU.add),
                 inc=svv, drain=True)
            prep_ta = S.n(svv)
            S.op("vector", lambda e: e.tensor_tensor(tA[4], tA[1], eps5, ALU.add), inc=svv, drain=True)
            vst = S.n(svv)
            S.op("scalar", lambda e: e.activation(tA[4], tA[4], ACT.Ln), waits=[(svv, vst)])
            S.op("scalar", lambda e: e.activation(invc, tA[4], ACT.Exp, scale=-0.5), inc=sss)
            sst = S.n(sss)
            S.op("vector", lambda e: e.tensor_scalar(c2a, tA[0], invc, -1.0, ALU.mult, ALU.mult),
                 waits=[(sss, sst)], drain=True)
            lt = a16s(P_lt, 2048)
            S.op("vector", lambda e, xbuf=xbuf, lt=lt: e.tensor_scalar(lt, xbuf, invc, c2a, ALU.mult, ALU.add),
                 waits=[(smm, prep_mm)], inc=svv, drain=True)
            prep_free[tti % 2] = S.n(svv)
            v_ln = S.n(svv)
            for mc in range(16):
                S.op("tensor", lambda e, lt=lt, mc=mc:
                     e.transpose(psT[:, mc * 128:(mc + 1) * 128], lt[:, mc * 128:(mc + 1) * 128], ident[:]),
                     waits=[(svv, v_ln)] if mc == 0 else [],
                     inc=smm if mc == 15 else None)
            prep_mm = S.n(smm)
            lt2 = a16s(P_lt + 2048, 2048)
            S.op("vector", lambda e, lt2=lt2: e.tensor_copy(lt2, psT[:, 0:2048]),
                 waits=[(smm, prep_mm), (dX, p3.get("lt2st", 0))], inc=svv)
            v_lt2 = S.n(svv)
            p3["xprep"] = S.n(svv)
            S.op("scalar", lambda e, lt2=lt2, tti=tti:
                 e.dma_start(out=ln2v[:, :, tti * 128:(tti + 1) * 128],
                             in_=lt2.rearrange("p (a t) -> p a t", t=128)),
                 waits=[(svv, v_lt2)], inc=dX, amt=16)
            p3["lt2st"] = S.n(dX)
        p3["x1f"] = S.n(svv)
        ln2_done = S.n(dX)

        for g in range(NG):
            src = [x1s, x1s2][g % 2]
            dst = [x1s2, x1s][g % 2]
            for i in range(GH):
                hd = g * GH + i
                S.op("gpsimd", lambda e, hd=hd, i=i:
                     e.dma_start(out=a16s(P_wo + i * 2048, 2048), in_=wo_in[hd]),
                     waits=[(svv, p3["wo_free"]), (smm, p3["wqkv_free"])], inc=dG, amt=16)
            wo_d = S.n(dG)
            for pr in range(2):
                for j in range(2):
                    hd = g * GH + pr * 2 + j
                    for wi, wt in enumerate([wq_in, wk_in, wv_in]):
                        dstsb = a16s(P_wqkv + (j * 3 + wi) * 2048, 2048)
                        S.op("gpsimd", lambda e, wt=wt, hd=hd, d=dstsb:
                             e.dma_start(out=d.rearrange("p (a d) -> p a d", d=128),
                                         in_=wt[:, hd, :].rearrange("(a p) d -> p a d", p=128)),
                             waits=[(smm, p3["wqkv_free"])], inc=dG, amt=16)
                wqkv_d = S.n(dG)
                for tc in range(4):
                    lnb = a16s(P_ln2c + (tc % 2) * 8192, 8192)
                    S.op("gpsimd", lambda e, tc=tc, lnb=lnb:
                         e.dma_start(out=lnb.rearrange("p (a t) -> p a t", t=512),
                                     in_=ln2v[:, :, tc * 512:(tc + 1) * 512]),
                         waits=[(dX, ln2_done), (smm, p3["ln%d" % (tc % 2)])], inc=dG, amt=16)
                    ln_d = S.n(dG)
                    for j in range(2):
                        i = pr * 2 + j
                        for wi in range(3):
                            wsb = a16s(P_wqkv + (j * 3 + wi) * 2048, 2048)
                            pdst = ps2[:, 0:512]
                            for mc in range(16):
                                S.op("tensor", lambda e, wsb=wsb, mc=mc, lnb=lnb, pdst=pdst:
                                     e.matmul(pdst, wsb[:, mc * 128:(mc + 1) * 128],
                                              lnb[:, mc * 512:(mc + 1) * 512],
                                              start=(mc == 0), stop=(mc == 15), skip_group_check=True),
                                     waits=([(dG, max(ln_d, wqkv_d)), (svv, p3["proj_ev"])] if mc == 0 else []),
                                     inc=smm if mc == 15 else None)
                            mm_p = S.n(smm)
                            if wi < 2:
                                kf = a32s(B_kq, 512)
                                S.op("vector", lambda e, kf=kf, pdst=pdst: e.tensor_copy(kf, pdst),
                                     waits=[(smm, mm_p)], inc=svv)
                                v_kf = S.n(svv)
                                S.op("tensor", lambda e, kf=kf: e.matmul(ps2[:, 512:1024], perm[:], kf,
                                     start=True, stop=True, skip_group_check=True),
                                     waits=[(svv, v_kf)], inc=smm)
                                mm_pk = S.n(smm)
                                S.op("vector", lambda e, kf=kf, tc=tc:
                                     e.tensor_tensor(kf, kf, cc_t[:, tc * 512:(tc + 1) * 512], ALU.mult),
                                     waits=[(smm, mm_pk)], drain=True)
                                S.op("vector", lambda e, tc=tc:
                                     e.tensor_tensor(a32s(B_kq + 512, 512), ps2[:, 512:1024],
                                                     ss_t[:, tc * 512:(tc + 1) * 512], ALU.mult), drain=True)
                                dsts = a16s([P_Q, P_K][wi] + i * 2048 + tc * 512, 512)
                                S.op("vector", lambda e, kf=kf, d=dsts:
                                     e.tensor_tensor(d, kf, a32s(B_kq + 512, 512), ALU.add), inc=svv, drain=True)
                                p3["proj_ev"] = S.n(svv)
                            else:
                                vt = a16s(P_vt, 512)
                                S.op("vector", lambda e, vt=vt, pdst=pdst: e.tensor_copy(vt, pdst),
                                     waits=[(smm, mm_p)], inc=svv)
                                v_vt = S.n(svv)
                                for b in range(4):
                                    S.op("tensor", lambda e, vt=vt, b=b:
                                         e.transpose(psT[:, b * 128:(b + 1) * 128], vt[:, b * 128:(b + 1) * 128], ident[:]),
                                         waits=[(svv, v_vt)] if b == 0 else [],
                                         inc=smm if b == 3 else None)
                                mm_vt = S.n(smm)
                                S.op("vector", lambda e, i=i, tc=tc:
                                     e.tensor_copy(a16s(P_V + i * 2048 + tc * 512, 512), psT[:, 0:512]),
                                     waits=[(smm, mm_vt)], inc=svv)
                                p3["proj_ev"] = S.n(svv)
                    p3["ln%d" % (tc % 2)] = S.n(smm)
                p3["wqkv_free"] = S.n(smm)
            proj_v = S.n(svv)

            for it in range(NT):
                jmax = (it + 1) * 128
                x1b = a32s(B_x1, 2048)
                S.op("scalar", lambda e, it=it, src=src, x1b=x1b:
                     e.dma_start(out=x1b, in_=src[it * 128:(it + 1) * 128, :]),
                     waits=[(svv, p3["x1f"]), (dX, p3.get("x1st", 0))], inc=dX, amt=16)
                x1d = S.n(dX)

                def plan_S(i, it=it, jmax=jmax):
                    nj = (jmax + 511) // 512
                    for jc in range(nj):
                        w = min(512, jmax - jc * 512)
                        S.op("tensor", lambda e, i=i, it=it, jc=jc, w=w:
                             e.matmul(ps1[:, jc * 512:jc * 512 + w],
                                      a16s(P_Q + i * 2048 + it * 128, 128),
                                      a16s(P_K + i * 2048 + jc * 512, w),
                                      start=True, stop=True, skip_group_check=True),
                             waits=([(svv, max(proj_v, p3["rev"])), (sss, p3["exp"])] if jc == 0 else []),
                             inc=smm if jc == nj - 1 else None)
                    return S.n(smm)

                def plan_mask_exp(i, mm_si, it=it, jmax=jmax):
                    pp = i % 2
                    S.op("vector", lambda e, it=it: e.tensor_tensor(
                        ps1[:, it * 128:(it + 1) * 128], ps1[:, it * 128:(it + 1) * 128], mask[:], ALU.add),
                        waits=[(smm, mm_si)], inc=svv)
                    v_mask = S.n(svv)
                    S.op("scalar", lambda e, jmax=jmax, pp=pp: e.activation(
                        a16s(P_Pa[pp], jmax), ps1[:, 0:jmax], ACT.Exp, scale=2.0 ** -7,
                        accum_out=den_p[pp]),
                        waits=[(svv, v_mask), (svv, p3["pn%d" % pp])], inc=sss)
                    p3["exp"] = S.n(sss)
                    return S.n(sss)

                mm_s = plan_S(0)
                s_exp = plan_mask_exp(0, mm_s)
                for i in range(GH):
                    pp = i % 2
                    Pa, Pb, PT = P_Pa[pp], P_Pb[pp], P_PT[pp]
                    if i + 1 < GH:
                        s_exp_next = plan_mask_exp(i + 1, plan_S(i + 1))
                    S.op("vector", lambda e, pp=pp: e.reciprocal(rdenp[pp], den_p[pp]),
                         waits=[(sss, s_exp)], drain=True)
                    S.op("vector", lambda e, jmax=jmax, Pa=Pa, Pb=Pb, pp=pp: e.tensor_scalar(
                        a16s(Pb, jmax), a16s(Pa, jmax), rdenp[pp], None, ALU.mult), inc=svv, drain=True)
                    p3["pn%d" % pp] = S.n(svv)
                    v_pn = S.n(svv)
                    if i + 1 < GH:
                        s_exp = s_exp_next
                    for jt in range(it + 1):
                        S.op("tensor", lambda e, jt=jt, Pb=Pb:
                             e.transpose(psT[:, jt * 128:(jt + 1) * 128],
                                         a16s(Pb + jt * 128, 128), ident[:]),
                             waits=([(svv, max(v_pn, p3["pt"]))] if jt == 0 else []),
                             inc=smm if jt == it else None)
                    mm_pt = S.n(smm)
                    S.op("vector", lambda e, jmax=jmax, PT=PT: e.tensor_copy(a16s(PT, jmax), psT[:, 0:jmax]),
                         waits=[(smm, mm_pt)], inc=svv)
                    p3["pt"] = S.n(svv)
                    v_pt = S.n(svv)
                    for jt in range(it + 1):
                        S.op("tensor", lambda e, i=i, jt=jt, PT=PT:
                             e.matmul(ps2[:, 512:640], a16s(P_V + i * 2048 + jt * 128, 128),
                                      a16s(PT + jt * 128, 128),
                                      start=(jt == 0), stop=(jt == it), skip_group_check=True),
                             waits=([(svv, max(v_pt, p3["oev"]))] if jt == 0 else []),
                             inc=smm if jt == it else None)
                    mm_pv = S.n(smm)
                    S.op("vector", lambda e, i=i: e.tensor_copy(a16s(P_O + i * 128, 128), ps2[:, 512:640]),
                         waits=[(smm, mm_pv)], inc=svv)
                    p3["oev"] = S.n(svv)
                o_ev = S.n(svv)
                mm_rc = []
                for c in range(4):
                    for i in range(GH):
                        S.op("tensor", lambda e, i=i, c=c:
                             e.matmul(ps1[:, c * 512:(c + 1) * 512], a16s(P_O + i * 128, 128),
                                      a16s(P_wo + i * 2048 + c * 512, 512),
                                      start=(i == 0), stop=(i == GH - 1), skip_group_check=True),
                             waits=([(svv, o_ev), (dG, wo_d), (sss, p3["exp"])] if (i == 0 and c == 0) else []),
                             inc=smm if i == GH - 1 else None)
                    mm_rc.append(S.n(smm))
                for c in range(4):
                    S.op("vector", lambda e, c=c, x1b=x1b:
                         e.tensor_tensor(x1b[:, c * 512:(c + 1) * 512],
                                         ps1[:, c * 512:(c + 1) * 512],
                                         x1b[:, c * 512:(c + 1) * 512], ALU.add),
                         waits=[(smm, mm_rc[c])] + ([(dX, x1d)] if c == 0 else []), inc=svv)
                p3["rev"] = S.n(svv)
                p3["x1f"] = S.n(svv)
                v_r = S.n(svv)
                S.op("scalar", lambda e, it=it, dst=dst, x1b=x1b:
                     e.dma_start(out=dst[it * 128:(it + 1) * 128, :], in_=x1b),
                     waits=[(svv, v_r)], inc=dX, amt=16)
                p3["x1st"] = S.n(dX)
            p3["wo_free"] = S.n(svv)
            p3["wqkv_free"] = S.n(smm)

        attn_done = S.n(dX)
        S.op("scalar", lambda e: e.nop(), waits=[(dX, attn_done)])

        ffn_pass(x1s, out_d, "f2")

        S.op("sync", lambda e: e.nop(), waits=[(dO, S.n(dO))])

        @block.sync
        def _(sync):
            S.emit("sync", sync)

        @block.vector
        def _(vector):
            S.emit("vector", vector)

        @block.scalar
        def _(scalar):
            S.emit("scalar", scalar)

        @block.tensor
        def _(tensor):
            S.emit("tensor", tensor)

        @block.gpsimd
        def _(gpsimd):
            S.emit("gpsimd", gpsimd)

    return nc


def _host_consts():
    ident = np.eye(128, dtype=bf16)
    perm = np.zeros((128, 128), np.float32)
    perm[(np.arange(128) + 64) % 128, np.arange(128)] = 1.0
    i = np.arange(128)[:, None]
    j = np.arange(128)[None, :]
    mask = np.where(j > i, np.float32(-1e30) * np.float32(128.0), np.float32(0.0)).astype(np.float32)
    half = 64
    freqs = np.power(np.float32(10000.0), -(np.arange(half, dtype=np.float32) / np.float32(half))).astype(np.float32)
    pos = np.arange(T, dtype=np.float32)[:, None]
    rad = (pos * freqs[None, :]).astype(np.float32)
    cos = np.cos(rad).astype(np.float32)
    sin = np.sin(rad).astype(np.float32)
    cc = np.ascontiguousarray(np.concatenate([cos.T, cos.T], axis=0).astype(np.float32))
    ss = np.ascontiguousarray(np.concatenate([sin.T, sin.T], axis=0).astype(np.float32))
    return ident, perm, mask, cc, ss


_CACHE = {}


def kernel(L0_attn_x, L0_attn_wq, L0_attn_wk, L0_attn_wv, L0_attn_wo,
           L0_ffn_w1, L0_ffn_w2, L0_ffn_w3):
    if "nc" not in _CACHE:
        _CACHE["nc"] = build_nc()
    nc = _CACHE["nc"]
    ident, perm, mask, cc, ss = _host_consts()
    w1t = np.ascontiguousarray(np.asarray(L0_ffn_w1, np.float32).T)
    w2t = np.ascontiguousarray(np.asarray(L0_ffn_w2, np.float32).T)
    w3t = np.ascontiguousarray(np.asarray(L0_ffn_w3, np.float32).T)
    wq = np.ascontiguousarray(np.asarray(L0_attn_wq, np.float32))
    wk = np.ascontiguousarray(np.asarray(L0_attn_wk, np.float32))
    wv = np.ascontiguousarray(np.asarray(L0_attn_wv, np.float32))
    wo = np.ascontiguousarray(np.asarray(L0_attn_wo, np.float32))
    x = np.asarray(L0_attn_x, np.float32)
    in_maps = []
    for c in range(8):
        in_maps.append({
            "x": np.ascontiguousarray(x[c]),
            "wq": wq, "wk": wk, "wv": wv, "wo": wo,
            "w1t": w1t, "w2t": w2t, "w3t": w3t,
            "ident": ident, "perm": perm, "mask": mask, "cch": cc, "ssh": ss,
        })
    res = run_bass_kernel_spmd(nc, in_maps, core_ids=list(range(8)))
    return np.stack([r["out"] for r in res.results]).astype(np.float32)
